# version 03
# baseline (speedup 1.0000x reference)
import numpy as np
import concourse.bass as bass
import concourse.mybir as mybir
from concourse.alu_op_type import AluOpType as ALU
from contextlib import ExitStack

F32 = mybir.dt.float32
BF16 = mybir.dt.bfloat16
I32 = mybir.dt.int32
I8 = mybir.dt.int8
AF = mybir.ActivationFunctionType
AX = mybir.AxisListType

D = 4096
TPC = 2048
TC = 512
NTC = TPC // TC
KT = D // 128
LNEG = -1.0e30


class B:
    def __init__(self, nc):
        self.nc = nc
        self.es = ExitStack()
        self._consts = {}

    def sb(self, name, shape, dtype):
        return self.es.enter_context(self.nc.sbuf_tensor(name, shape, dtype))

    def ps(self, name, shape, dtype=F32):
        return self.es.enter_context(self.nc.psum_tensor(name, shape, dtype))

    def sem(self, name):
        return self.es.enter_context(self.nc.semaphore(name))

    def const(self, name, arr):
        if name not in self._consts:
            self._consts[name] = self.nc.inline_tensor(np.ascontiguousarray(arr), name=name)
        return self._consts[name]


def split_dma(eng, out_ap, in_ap, sem=None, parts=4, dim=0):
    n = in_ap.shape[dim]
    assert out_ap.shape[dim] == n, (out_ap.shape, in_ap.shape, dim)
    parts = min(parts, n)
    step = (n + parts - 1) // parts
    cnt = 0
    for s0 in range(0, n, step):
        s1 = min(n, s0 + step)
        sl = [slice(None)] * len(in_ap.shape)
        sl[dim] = slice(s0, s1)
        ins = eng.dma_start(out=out_ap[tuple(sl)], in_=in_ap[tuple(sl)])
        if sem is not None:
            ins.then_inc(sem, 16)
        cnt += 1
    return cnt


def build_s4(nc, name="s4", xr_name="xr", xo_name="xo", make_io=True, b=None,
             io=None):
    if b is None:
        b = B(nc)
    if io is None:
        io = {}
        io["xr"] = nc.dram_tensor(xr_name, [D, TPC], F32, kind="ExternalInput").ap()
        io["WT"] = nc.dram_tensor("l1f_WT", [D, D], F32, kind="ExternalInput").ap()
        io["AT"] = nc.dram_tensor("l1f_AT", [D, 256], F32, kind="ExternalInput").ap()
        io["BT"] = nc.dram_tensor("l1f_BT", [256, D], F32, kind="ExternalInput").ap()
        io["b_pm"] = nc.dram_tensor("l1f_b_pm", [128, KT], F32, kind="ExternalInput").ap()
        io["p_rep"] = nc.dram_tensor("l1f_p_rep", [128, 2], F32, kind="ExternalInput").ap()
        io["xo"] = nc.dram_tensor(xo_name, [D, TPC], F32, kind="ExternalOutput").ap()

    xr, WT, AT, BT, b_pm, p_rep, xo = (io[k] for k in
                                       ["xr", "WT", "AT", "BT", "b_pm", "p_rep", "xo"])

    xc = b.sb(name + "_xc", [128, KT * TC], F32)
    xlnb = b.sb(name + "_xln", [128, KT * TC], BF16)
    sq = b.sb(name + "_sq", [128, KT * TC], BF16)
    wbuf = [b.sb(f"{name}_w{i}", [128, KT * 128], BF16) for i in range(2)]
    abuf = b.sb(name + "_at", [128, KT * 256], BF16)
    bbuf = b.sb(name + "_bt", [128, 2 * D], BF16)
    bpm = b.sb(name + "_bpm", [128, KT], F32)
    halfb = b.sb(name + "_halfb", [128, KT], F32)
    prep = b.sb(name + "_prep", [128, 2], F32)
    hsb = b.sb(name + "_h", [128, 2 * TC], BF16)
    stats = b.sb(name + "_stats", [1, 6 * TC], F32)
    tmps = [b.sb(f"{name}_tmp{i}", [128, TC], F32) for i in range(2)]
    xout = [b.sb(f"{name}_xout{i}", [128, TC], F32) for i in range(2)]
    vtmp = b.sb(name + "_vtmp", [128, TC], F32)

    sum_ps = b.ps(name + "_sum", [1, TC])
    ssq_ps = b.ps(name + "_ssq", [1, TC])
    mb_ps = b.ps(name + "_mb", [128, TC])
    rb_ps = b.ps(name + "_rb", [128, TC])
    h_ps = b.ps(name + "_hps", [128, 2 * TC])
    o_ps = [b.ps(f"{name}_ops{i}", [128, TC]) for i in range(2)]

    onescol = b.const("onescol", np.ones((128, 1), np.float32).astype(np.float32))
    onesrow = b.const("onesrow", np.ones((1, 128), np.float32))
    onescol_sb = b.sb(name + "_oc", [128, 1], BF16)
    onesrow_sb = b.sb(name + "_or", [1, 128], F32)

    sy = b.sem(name + "_sy")
    gp = b.sem(name + "_gp")
    gw = [b.sem(name + "_gw%d" % i) for i in range(2)]
    sA = b.sem(name + "_sA")
    sS1 = b.sem(name + "_sS1")
    sT1 = b.sem(name + "_sT1")
    sV1 = b.sem(name + "_sV1")
    sS2 = b.sem(name + "_sS2")
    sT2 = b.sem(name + "_sT2")
    sV2 = b.sem(name + "_sV2")
    sT3 = b.sem(name + "_sT3")
    sS3 = b.sem(name + "_sS3")
    sT4 = b.sem(name + "_sT4")
    sS4 = b.sem(name + "_sS4")
    sV3 = b.sem(name + "_sV3")
    syo = [b.sem(name + "_syo%d" % i) for i in range(2)]

    NPRE = 8

    blk = b.es.enter_context(nc.Block())

    def body_sync(sync):
        for c in range(NTC):
            if c > 0:
                sync.wait_ge(sV3, 32 * c)
            split_dma(sync, xc.rearrange("p (k w) -> p k w", k=KT),
                      xr.rearrange("(k p) t -> p k t", p=128)[:, :, c * TC:(c + 1) * TC],
                      sem=sy, parts=4, dim=1)
            for o in range(KT):
                sync.wait_ge(sV3, 32 * c + o + 1)
                sync.dma_start(out=xo.rearrange("(k p) t -> k p t", p=128)[o, :, c * TC:(c + 1) * TC],
                               in_=xout[o % 2][:, :]).then_inc(syo[o % 2], 16)

    def body_gpsimd(g):
        split_dma(g, abuf.rearrange("p (k w) -> p k w", k=KT),
                  AT.rearrange("(k p) n -> p k n", p=128), sem=gp, parts=2, dim=1)
        split_dma(g, bbuf.rearrange("p (k w) -> p k w", k=2),
                  BT.rearrange("(k p) n -> p k n", p=128), sem=gp, parts=2, dim=1)
        g.dma_start(out=bpm[:, :], in_=b_pm).then_inc(gp, 16)
        g.dma_start(out=prep[:, :], in_=p_rep).then_inc(gp, 16)
        g.dma_start(out=onescol_sb[:, :], in_=onescol.ap()).then_inc(gp, 16)
        g.dma_start(out=onesrow_sb[:, :], in_=onesrow.ap()).then_inc(gp, 16)
        for c in range(NTC):
            for o in range(KT):
                i = c * KT + o
                if i >= 2:
                    g.wait_ge(sT4, i - 1)
                split_dma(g, wbuf[o % 2].rearrange("p (k w) -> p k w", k=KT),
                          WT.rearrange("(k p) n -> p k n", p=128)[:, :, o * 128:(o + 1) * 128],
                          sem=gw[o % 2], parts=4, dim=1)

    def body_vector(vector):
        vector.wait_ge(gp, 16 * NPRE)
        vector.tensor_scalar_mul(halfb[:, :], bpm[:, :], 0.5)
        for c in range(NTC):
            vector.wait_ge(sy, 64 * (c + 1))
            for k in range(KT):
                ins = vector.tensor_copy(xlnb[:, k * TC:(k + 1) * TC], xc[:, k * TC:(k + 1) * TC])
            ins.then_inc(sA, 1)
            vector.wait_ge(sT1, c + 1)
            s_sum = stats[:, 0 * TC:1 * TC]
            s_ssq = stats[:, 1 * TC:2 * TC]
            s_mean = stats[:, 2 * TC:3 * TC]
            s_var = stats[:, 3 * TC:4 * TC]
            s_rstd = stats[:, 4 * TC:5 * TC]
            s_tmp = stats[:, 5 * TC:6 * TC]
            vector.tensor_scalar_mul(s_mean, sum_ps[:, :], 1.0 / D)
            vector.tensor_scalar_mul(s_ssq, ssq_ps[:, :], 1.0 / D)
            vector.drain()
            vector.tensor_mul(s_tmp, s_mean, s_mean)
            vector.drain()
            vector.tensor_sub(s_var, s_ssq, s_tmp)
            vector.drain()
            ins = vector.tensor_scalar_add(s_var, s_var, 1e-5)
            ins.then_inc(sV1, 1)
            vector.wait_ge(sT2, c + 1)
            for k in range(KT):
                vector.tensor_sub(vtmp[:, :], xc[:, k * TC:(k + 1) * TC], mb_ps[:, :])
                vector.drain()
                ins = vector.tensor_mul(xlnb[:, k * TC:(k + 1) * TC], vtmp[:, :], rb_ps[:, :])
                vector.drain()
            ins.then_inc(sV2, 1)
            for o in range(KT):
                i = c * KT + o
                vector.wait_ge(sS4, i + 1)
                if i >= 2:
                    vector.wait_ge(syo[o % 2], 16 * (i // 2))
                ins = vector.tensor_add(xout[o % 2][:, :], tmps[o % 2][:, :],
                                        xc[:, o * TC:(o + 1) * TC])
                ins.then_inc(sV3, 1)

    def body_scalar(scalar):
        for c in range(NTC):
            scalar.wait_ge(sA, c + 1)
            for k in range(KT):
                ins = scalar.activation(sq[:, k * TC:(k + 1) * TC],
                                        xlnb[:, k * TC:(k + 1) * TC], AF.Square)
            ins.then_inc(sS1, 1)
            scalar.wait_ge(sV1, c + 1)
            s_var = stats[:, 3 * TC:4 * TC]
            s_rstd = stats[:, 4 * TC:5 * TC]
            s_tmp = stats[:, 5 * TC:6 * TC]
            scalar.activation(s_tmp, s_var, AF.Ln)
            scalar.drain()
            ins = scalar.activation(s_rstd, s_tmp, AF.Exp, scale=-0.5)
            ins.then_inc(sS2, 1)
            scalar.wait_ge(sT3, c + 1)
            scalar.mul(hsb[:, 0:TC], h_ps[:, 0:TC], prep[:, 0:1])
            ins = scalar.mul(hsb[:, TC:2 * TC], h_ps[:, TC:2 * TC], prep[:, 1:2])
            ins.then_inc(sS3, 1)
            for o in range(KT):
                i = c * KT + o
                scalar.wait_ge(sT4, i + 1)
                if i >= 2:
                    scalar.wait_ge(sV3, i - 1)
                ins = scalar.activation(tmps[o % 2][:, :], o_ps[o % 2][:, :],
                                        AF.Identity, bias=halfb[:, o:o + 1], scale=0.5)
                ins.then_inc(sS4, 1)

    def body_tensor(tensor):
        tensor.wait_ge(gp, 16 * NPRE)
        for c in range(NTC):
            tensor.wait_ge(sS1, c + 1)
            for k in range(KT):
                tensor.matmul(sum_ps[:, :], onescol_sb[:, :], xlnb[:, k * TC:(k + 1) * TC],
                              start=(k == 0), stop=(k == KT - 1))
            for k in range(KT):
                ins = tensor.matmul(ssq_ps[:, :], onescol_sb[:, :], sq[:, k * TC:(k + 1) * TC],
                                    start=(k == 0), stop=(k == KT - 1))
            ins.then_inc(sT1, 1)
            tensor.wait_ge(sS2, c + 1)
            tensor.matmul(mb_ps[:, :], onesrow_sb[:, :], stats[:, 2 * TC:3 * TC],
                          start=True, stop=True)
            ins = tensor.matmul(rb_ps[:, :], onesrow_sb[:, :], stats[:, 4 * TC:5 * TC],
                                start=True, stop=True)
            ins.then_inc(sT2, 1)
            tensor.wait_ge(sV2, c + 1)
            for e in range(2):
                for k in range(KT):
                    ins = tensor.matmul(h_ps[:, e * TC:(e + 1) * TC],
                                        abuf[:, (k * 2 + e) * 128:(k * 2 + e) * 128 + 128],
                                        xlnb[:, k * TC:(k + 1) * TC],
                                        start=(k == 0), stop=(k == KT - 1))
            ins.then_inc(sT3, 1)
            for o in range(KT):
                i = c * KT + o
                tensor.wait_ge(gw[o % 2], 16 * 4 * (i // 2 + 1))
                if i >= 2:
                    tensor.wait_ge(sS4, i - 1)
                if o == 0:
                    tensor.wait_ge(sS3, c + 1)
                p = o_ps[o % 2]
                for k in range(KT):
                    tensor.matmul(p[:, :], wbuf[o % 2][:, k * 128:(k + 1) * 128],
                                  xlnb[:, k * TC:(k + 1) * TC],
                                  start=(k == 0), stop=False)
                for e in range(2):
                    ins = tensor.matmul(p[:, :], bbuf[:, (e * KT + o) * 128:(e * KT + o) * 128 + 128],
                                        hsb[:, e * TC:(e + 1) * TC],
                                        start=False, stop=(e == 1))
                ins.then_inc(sT4, 1)

    def body_sync2(sync):
        sync.wait_ge(syo[0], 16 * KT * NTC // 2)
        sync.wait_ge(syo[1], 16 * KT * NTC // 2)

    blk.sync(body_sync)
    blk.gpsimd(body_gpsimd)
    blk.vector(body_vector)
    blk.scalar(body_scalar)
    blk.tensor(body_tensor)
    blk.sync(body_sync2)
    return b, io




NCH_GU = 43
NCH_DN = 32
HKT = 86


def build_s1(nc, name="s1", io=None, mode="full"):
    deq = mode in ("full", "deq")
    gn = mode in ("full", "gn")
    b = B(nc)
    if io is None:
        io = {}
        io["xr"] = nc.dram_tensor(name + "_xr", [D, TPC], F32, kind="ExternalInput").ap()
        io["qw8_gu"] = nc.dram_tensor("ffnq_qw8_gu", [D, 11008], I8, kind="ExternalInput").ap()
        io["qz8_gu"] = nc.dram_tensor("ffnq_qz8_gu", [32, 11008], I8, kind="ExternalInput").ap()
        io["s_gu"] = nc.dram_tensor("ffnq_s_gu", [32, 22016], F32, kind="ExternalInput").ap()
        io["qw8_dn"] = nc.dram_tensor("ffnq_qw8_dn", [11008, 2048], I8, kind="ExternalInput").ap()
        io["qz8_dn"] = nc.dram_tensor("ffnq_qz8_dn", [86, 2048], I8, kind="ExternalInput").ap()
        io["s_dn"] = nc.dram_tensor("ffnq_s_dn", [86, D], F32, kind="ExternalInput").ap()
        io["xo"] = nc.dram_tensor(name + "_xo", [D, TPC], F32, kind="ExternalOutput").ap()
    xr, qw8_gu, qz8_gu, s_gu, qw8_dn, qz8_dn, s_dn, xo = (
        io[k] for k in ["xr", "qw8_gu", "qz8_gu", "s_gu", "qw8_dn", "qz8_dn", "s_dn", "xo"])

    wkind = {"full": "Internal", "deq": "ExternalOutput", "gn": "ExternalInput"}[mode]
    Wgu = nc.dram_tensor(name + "_Wgu", [32, 128, 22016], BF16, kind=wkind).ap()
    Wdn = nc.dram_tensor(name + "_Wdn", [HKT, 128, D], BF16, kind=wkind).ap()

    xs2 = [b.sb(f"{name}_xs{i}", [128, TC], F32) for i in range(2)]
    xres = [b.sb(f"{name}_xr{i}", [128, TC], F32) for i in range(2)]
    xout = [b.sb(f"{name}_xo{i}", [128, TC], F32) for i in range(2)]
    silu_t = [b.sb(f"{name}_silu{i}", [128, TC], F32) for i in range(2)]
    vtmp = b.sb(name + "_vtmp", [128, TC], F32)
    stats = b.sb(name + "_stats", [1, 3 * TC], F32)
    oc_sb = b.sb(name + "_ocs", [128, 1], BF16)
    or_sb = b.sb(name + "_ors", [1, 128], F32)

    des = ExitStack()
    qwb = [des.enter_context(nc.sbuf_tensor(f"{name}_qw{i}", [128, 11008], I8)) for i in range(2)]
    ext8 = des.enter_context(nc.sbuf_tensor(name + "_ext8", [128, 512], I8))
    extf = des.enter_context(nc.sbuf_tensor(name + "_extf", [128, 512], F32))
    dtmp = des.enter_context(nc.sbuf_tensor(name + "_dtmp", [128, 512], F32))
    dstage = [des.enter_context(nc.sbuf_tensor(f"{name}_dst{i}", [128, 4 * 512], BF16)) for i in range(2)]
    s_bf = des.enter_context(nc.sbuf_tensor(name + "_sbf", [32, 22016], BF16))
    z_bf = des.enter_context(nc.sbuf_tensor(name + "_zbf", [32, 22016], BF16))
    g32_sb = des.enter_context(nc.sbuf_tensor(name + "_g32s", [32, 32 * 128], BF16))
    g86_sb = des.enter_context(nc.sbuf_tensor(name + "_g86s", [86, 86 * 128], BF16))
    sdn_bf = des.enter_context(nc.sbuf_tensor(name + "_sdnbf", [86, D], BF16))
    zdn_bf = des.enter_context(nc.sbuf_tensor(name + "_zdnbf", [86, D], BF16))
    zgu8 = des.enter_context(nc.sbuf_tensor(name + "_zgu8", [32, 11008], I8))
    zdn8 = des.enter_context(nc.sbuf_tensor(name + "_zdn8", [86, 2048], I8))
    ze8 = des.enter_context(nc.sbuf_tensor(name + "_ze8", [86, 2048], I8))
    sq32 = des.enter_context(nc.sbuf_tensor(name + "_sq32", [86, 512], F32))
    qwb = [t[:, :] for t in qwb]
    ext8, extf, dtmp, s_bf, z_bf = ext8[:, :], extf[:, :], dtmp[:, :], s_bf[:, :], z_bf[:, :]
    dstage = [t[:, :] for t in dstage]
    g32_sb, g86_sb = g32_sb[:, :], g86_sb[:, :]
    sdn_bf, zdn_bf = sdn_bf[:, :], zdn_bf[:, :]
    zgu8, zdn8, ze8, sq32 = zgu8[:, :], zdn8[:, :], ze8[:, :], sq32[:, :]
    des.close()

    xlnb = b.sb(name + "_xln", [128, KT * TC], BF16)
    wbuf = [b.sb(f"{name}_w{i}", [128, KT * 512], BF16) for i in range(2)]
    sqv = wbuf[1]
    dnw = [w[:, 0:HKT * 128] for w in wbuf]
    hbuf = b.sb(name + "_h", [128, HKT * TC], BF16)

    g32a = np.zeros((32, 32, 128), np.float32)
    for g in range(32):
        g32a[g, g, :] = 1.0
    g86a = np.zeros((86, 86, 128), np.float32)
    for g in range(86):
        g86a[g, g, :] = 1.0
    g32_c = b.const(name + "_g32", g32a.transpose(1, 0, 2).reshape(32, 32 * 128))
    g86_c = b.const(name + "_g86", g86a.transpose(1, 0, 2).reshape(86, 86 * 128))
    onescol = b.const(name + "_onescol", np.ones((128, 1), np.float32))
    onesrow = b.const(name + "_onesrow", np.ones((1, 128), np.float32))

    sum_ps = b.ps(name + "_sum", [1, TC])
    ssq_ps = b.ps(name + "_ssq", [1, TC])
    cd_ps = [b.ps(f"{name}_cd{i}", [128, TC]) for i in range(2)]
    gu_ps = [b.ps(f"{name}_gu{i}", [128, TC]) for i in range(4)]
    mb_ps, rb_ps = cd_ps[0], cd_ps[1]

    zq = b.sem(name + "_zq"); zv = b.sem(name + "_zv")
    dq = [b.sem(name + "_dq%d" % i) for i in range(2)]
    dtg = b.sem(name + "_dtg"); dvc = b.sem(name + "_dvc")
    dw = [b.sem(name + "_dw%d" % i) for i in range(2)]
    lnc = b.sem(name + "_lnc")
    lnq = [b.sem(name + "_lnq%d" % i) for i in range(2)]
    lns = b.sem(name + "_lns")
    lnt = b.sem(name + "_lnt"); lnv = b.sem(name + "_lnv")
    wg = [b.sem(name + "_wg%d" % i) for i in range(2)]
    gm = b.sem(name + "_gm"); ge = b.sem(name + "_ge")
    sle = b.sem(name + "_sle")
    wd = [b.sem(name + "_wd%d" % i) for i in range(2)]
    dm = b.sem(name + "_dm"); de = b.sem(name + "_de")
    oq = [b.sem(name + "_oq%d" % i) for i in range(2)]
    xq = [b.sem(name + "_xq%d" % i) for i in range(2)]

    GRP = [list(range(4 * j, min(43, 4 * j + 4))) for j in range(11)]
    NPREQ = 6
    NDW = 11 * 32 + 2 * HKT

    blk = b.es.enter_context(nc.Block())

    def body_gpsimd(g):
        if deq:
            emit_gpsimd_deq(g)
        if gn:
            emit_gpsimd_gn(g)

    def emit_gpsimd_deq(g):
        g.dma_start(out=g32_sb, in_=g32_c.ap()).then_inc(zq, 16)
        g.dma_start(out=g86_sb, in_=g86_c.ap()).then_inc(zq, 16)
        g.dma_start(out=oc_sb[:, :], in_=onescol.ap()).then_inc(zq, 16)
        g.dma_start(out=or_sb[:, :], in_=onesrow.ap()).then_inc(zq, 16)
        g.dma_start(out=zgu8, in_=qz8_gu).then_inc(zq, 16)
        g.dma_start(out=zdn8, in_=qz8_dn).then_inc(zq, 16)
        for q in range(8):
            if q > 0:
                g.wait_ge(zv, q)
            g.dma_start(out=sq32, in_=s_dn[:, q * 512:(q + 1) * 512]).then_inc(zq, 16)
        for c in range(NCH_GU):
            g.wait_ge(zv, 8 + c)
            g.dma_start(out=sq32[0:32, 0:512],
                        in_=s_gu[:, c * 512:(c + 1) * 512]).then_inc(zq, 16)
    def emit_gpsimd_gn(g):
        g.dma_start(out=oc_sb[:, :], in_=onescol.ap()).then_inc(lnc, 16)
        g.dma_start(out=or_sb[:, :], in_=onesrow.ap()).then_inc(lnc, 16)
        for tc in range(NTC):
            if tc == 0 and deq:
                g.wait_ge(dw[0], 16 * (NDW // 2))
                g.wait_ge(dw[1], 16 * (NDW // 2))
            else:
                g.wait_ge(gm, 2 * NCH_GU * tc)
            for k in range(KT):
                g.dma_start(out=xlnb[:, k * TC:(k + 1) * TC],
                            in_=xr.rearrange("(k p) t -> k p t", p=128)[k, :, tc * TC:(tc + 1) * TC]
                            ).then_inc(lnc, 16)
            for c in range(NCH_DN):
                j = tc * NCH_DN + c
                g.wait_ge(de, j + 1)
                g.dma_start(out=xo.rearrange("(k p) t -> k p t", p=128)[c, :, tc * TC:(tc + 1) * TC],
                            in_=xout[c % 2][:, :]).then_inc(oq[j % 2], 16)

    def body_sync(sync):
        if deq:
            emit_sync_deq(sync)
        if gn:
            emit_sync_gn(sync)
        if gn:
            sync.wait_ge(oq[0], 16 * NTC * NCH_DN // 2)
            sync.wait_ge(oq[1], 16 * NTC * NCH_DN // 2)
        else:
            sync.wait_ge(dw[0], 16 * (NDW // 2))
            sync.wait_ge(dw[1], 16 * (NDW // 2))

    def emit_sync_deq(sync):
        for g in range(32):
            if g >= 2:
                sync.wait_ge(dvc, NCH_GU * (g - 1))
            sync.dma_start(out=qwb[g % 2],
                           in_=qw8_gu[g * 128:(g + 1) * 128, :]).then_inc(dq[g % 2], 16)
        for g in range(HKT):
            r = 32 + g
            if g < 2:
                sync.wait_ge(dvc, NCH_GU * (31 + g))
            else:
                sync.wait_ge(dvc, NCH_GU * 32 + 8 * (g - 1))
            sync.dma_start(out=qwb[r % 2][:, 0:2048],
                           in_=qw8_dn[g * 128:(g + 1) * 128, :]).then_inc(dq[r % 2], 16)
    def emit_sync_gn(sync):
        for tc in range(NTC):
            for k in range(KT):
                if 32 * tc + k >= 2:
                    sync.wait_ge(lnv, 33 * tc + k - (1 if k < 2 else 0))
                sync.dma_start(out=xs2[k % 2][:, :],
                               in_=xr.rearrange("(k p) t -> k p t", p=128)[k, :, tc * TC:(tc + 1) * TC]
                               ).then_inc(lnq[k % 2], 16)
            for c in range(NCH_GU):
                i = tc * NCH_GU + c
                if i == 0 and deq:
                    sync.wait_ge(dw[0], 16 * (NDW // 2))
                    sync.wait_ge(dw[1], 16 * (NDW // 2))
                if i >= 2:
                    sync.wait_ge(gm, 2 * i - 2)
                if tc > 0 and c in (0, 1):
                    sync.wait_ge(dm, NCH_DN * tc)
                if c == 1:
                    sync.wait_ge(lnt, 2 * tc + 1)
                for part in range(4):
                    sync.dma_start(
                        out=wbuf[i % 2].rearrange("p (k w) -> p k w", k=KT)[:, part * 8:(part + 1) * 8, :],
                        in_=Wgu.rearrange("g p n -> p g n")[:, part * 8:(part + 1) * 8, c * 512:(c + 1) * 512],
                    ).then_inc(wg[i % 2], 16)
            for c in range(NCH_DN):
                j = tc * NCH_DN + c
                if c in (0, 1):
                    sync.wait_ge(gm, 2 * NCH_GU * (tc + 1))
                if j >= 2:
                    sync.wait_ge(dm, j - 1)
                for part in range(4):
                    k0 = part * 22
                    k1 = (part + 1) * 22 if part < 3 else HKT
                    sync.dma_start(
                        out=dnw[c % 2].rearrange("p (k w) -> p k w", k=HKT)[:, k0:k1, :],
                        in_=Wdn.rearrange("g p n -> p g n")[:, k0:k1, c * 128:(c + 1) * 128],
                    ).then_inc(wd[j % 2], 16)
                if j >= 2:
                    sync.wait_ge(de, j - 1)
                sync.dma_start(out=xres[c % 2][:, :],
                               in_=xr.rearrange("(k p) t -> k p t", p=128)[c, :, tc * TC:(tc + 1) * TC]
                               ).then_inc(xq[j % 2], 16)

    def body_vector(vector):
        if deq:
            emit_vec_deq(vector)
        if gn:
            emit_vec_gn(vector)

    def emit_vec_deq(vector):
        A_ = ALU
        vector.wait_ge(zq, 16 * 7)
        for h in range(2):
            vector.tensor_scalar(ze8[0:86, 0:2048:2], zdn8[:, h * 1024:(h + 1) * 1024],
                                 15, None, A_.bitwise_and)
            vector.tensor_scalar(ze8[0:86, 1:2048:2], zdn8[:, h * 1024:(h + 1) * 1024],
                                 4, 15, A_.logical_shift_right, A_.bitwise_and)
            vector.drain()
            vector.tensor_copy(zdn_bf[:, h * 2048:(h + 1) * 2048], ze8[0:86, :])
            vector.drain()
        for q in range(8):
            vector.wait_ge(zq, 16 * (7 + q))
            ins = vector.tensor_copy(sdn_bf[:, q * 512:(q + 1) * 512], sq32)
            ins.then_inc(zv, 1)
        for c in range(NCH_GU):
            vector.wait_ge(zq, 16 * (15 + c))
            vector.tensor_scalar(ze8[0:32, 0:512:2], zgu8[:, c * 256:(c + 1) * 256],
                                 15, None, A_.bitwise_and)
            vector.tensor_scalar(ze8[0:32, 1:512:2], zgu8[:, c * 256:(c + 1) * 256],
                                 4, 15, A_.logical_shift_right, A_.bitwise_and)
            vector.drain()
            vector.tensor_copy(z_bf[:, c * 512:(c + 1) * 512], ze8[0:32, 0:512])
            vector.drain()
            ins = vector.tensor_copy(s_bf[:, c * 512:(c + 1) * 512], sq32[0:32, 0:512])
            ins.then_inc(zv, 1)
        for g in range(32):
            vector.wait_ge(dq[g % 2], 16 * (g // 2 + 1))
            for j, grp in enumerate(GRP):
                sg = 11 * g + j
                if sg >= 2:
                    vector.wait_ge(dw[sg % 2], 16 * (sg // 2))
                for ci, c in enumerate(grp):
                    gc = NCH_GU * g + c
                    vector.wait_ge(dtg, gc + 1)
                    vector.tensor_scalar(ext8[:, 0:512:2], qwb[g % 2][:, c * 256:(c + 1) * 256],
                                         15, None, A_.bitwise_and)
                    vector.tensor_scalar(ext8[:, 1:512:2], qwb[g % 2][:, c * 256:(c + 1) * 256],
                                         4, 15, A_.logical_shift_right, A_.bitwise_and)
                    vector.drain()
                    vector.tensor_copy(extf, ext8)
                    vector.drain()
                    vector.tensor_sub(dtmp, extf, gu_ps[2 * (gc % 2)][:, :])
                    vector.drain()
                    ins = vector.tensor_mul(dstage[sg % 2][:, ci * 512:(ci + 1) * 512],
                                            dtmp, gu_ps[2 * (gc % 2) + 1][:, :])
                    ins.then_inc(dvc, 1)
        for g in range(HKT):
            r = 32 + g
            vector.wait_ge(dq[r % 2], 16 * (r // 2 + 1))
            for h in range(2):
                sg = 11 * 32 + 2 * g + h
                vector.wait_ge(dw[sg % 2], 16 * (sg // 2))
                for q in range(4 * h, 4 * h + 4):
                    gc = NCH_GU * 32 + 8 * g + q
                    vector.wait_ge(dtg, gc + 1)
                    vector.tensor_scalar(ext8[:, 0:512:2], qwb[r % 2][:, q * 256:(q + 1) * 256],
                                         15, None, A_.bitwise_and)
                    vector.tensor_scalar(ext8[:, 1:512:2], qwb[r % 2][:, q * 256:(q + 1) * 256],
                                         4, 15, A_.logical_shift_right, A_.bitwise_and)
                    vector.drain()
                    vector.tensor_copy(extf, ext8)
                    vector.drain()
                    vector.tensor_sub(dtmp, extf, gu_ps[2 * (gc % 2)][:, :])
                    vector.drain()
                    ins = vector.tensor_mul(dstage[sg % 2][:, (q - 4 * h) * 512:(q - 4 * h + 1) * 512],
                                            dtmp, gu_ps[2 * (gc % 2) + 1][:, :])
                    ins.then_inc(dvc, 1)
    def emit_vec_gn(vector):
        A_ = ALU
        for tc in range(NTC):
            vector.wait_ge(lnt, 2 * tc + 1)
            s_mean = stats[:, 0 * TC:1 * TC]
            s_var = stats[:, 1 * TC:2 * TC]
            s_tmp = stats[:, 2 * TC:3 * TC]
            vector.tensor_scalar_mul(s_mean, sum_ps[:, :], 1.0 / D)
            vector.tensor_scalar_mul(s_var, ssq_ps[:, :], 1.0 / D)
            vector.drain()
            vector.tensor_mul(s_tmp, s_mean, s_mean)
            vector.drain()
            vector.tensor_sub(s_var, s_var, s_tmp)
            vector.drain()
            ins = vector.tensor_scalar_add(s_var, s_var, 1e-5)
            ins.then_inc(lnv, 1)
            vector.wait_ge(lnt, 2 * tc + 2)
            for k in range(KT):
                vector.wait_ge(lnq[k % 2], 16 * ((32 * tc + k) // 2 + 1))
                vector.tensor_sub(vtmp[:, :], xs2[k % 2][:, :], mb_ps[:, :])
                vector.drain()
                ins = vector.tensor_mul(xlnb[:, k * TC:(k + 1) * TC], vtmp[:, :], rb_ps[:, :])
                ins.then_inc(lnv, 1)
                vector.drain()
            for c in range(NCH_GU):
                i = tc * NCH_GU + c
                for P in range(2):
                    e = 2 * i + P
                    vector.wait_ge(sle, e + 1)
                    ins = vector.tensor_mul(hbuf[:, (2 * c + P) * TC:(2 * c + P + 1) * TC],
                                            gu_ps[2 * P + 1][:, :], silu_t[e % 2][:, :])
                    ins.then_inc(ge, 1)
            for c in range(NCH_DN):
                j = tc * NCH_DN + c
                vector.wait_ge(dm, j + 1)
                vector.wait_ge(xq[j % 2], 16 * (j // 2 + 1))
                if j >= 2:
                    vector.wait_ge(oq[j % 2], 16 * (j // 2))
                ins = vector.scalar_tensor_tensor(xout[c % 2][:, :], cd_ps[c % 2][:, :],
                                                  0.5, xres[c % 2][:, :], A_.mult, A_.add)
                ins.then_inc(de, 1)

    def emit_scalar_deq(scalar):
        for g in range(32):
            for j, grp in enumerate(GRP):
                sg = 11 * g + j
                scalar.wait_ge(dvc, NCH_GU * g + grp[-1] + 1)
                scalar.dma_start(out=Wgu[g, :, grp[0] * 512:(grp[-1] + 1) * 512],
                                 in_=dstage[sg % 2][:, 0:len(grp) * 512]).then_inc(dw[sg % 2], 16)
        for g in range(HKT):
            for h in range(2):
                sg = 11 * 32 + 2 * g + h
                scalar.wait_ge(dvc, NCH_GU * 32 + 8 * g + 4 * (h + 1))
                scalar.dma_start(out=Wdn[g, :, h * 2048:(h + 1) * 2048],
                                 in_=dstage[sg % 2][:, :]).then_inc(dw[sg % 2], 16)

    def body_scalar(scalar):
        if deq:
            emit_scalar_deq(scalar)
        if not gn:
            return
        for tc in range(NTC):
            if tc > 0:
                scalar.wait_ge(dm, NCH_DN * tc)
                scalar.wait_ge(gm, 2 * NCH_GU * tc)
            scalar.wait_ge(lnc, 16 * (2 + 32 * (tc + 1)))
            for k in range(KT):
                ins = scalar.activation(sqv[:, k * TC:(k + 1) * TC],
                                        xlnb[:, k * TC:(k + 1) * TC], AF.Square)
            ins.then_inc(lns, 1)
            scalar.wait_ge(lnv, 33 * tc + 1)
            s_var = stats[:, 1 * TC:2 * TC]
            s_tmp = stats[:, 2 * TC:3 * TC]
            scalar.activation(s_tmp, s_var, AF.Ln)
            scalar.drain()
            ins = scalar.activation(s_var, s_tmp, AF.Exp, scale=-0.5)
            ins.then_inc(lns, 1)
            for c in range(NCH_GU):
                i = tc * NCH_GU + c
                for P in range(2):
                    e = 2 * i + P
                    scalar.wait_ge(gm, e + 1)
                    if e >= 2:
                        scalar.wait_ge(ge, e - 1)
                    ins = scalar.activation(silu_t[e % 2][:, :], gu_ps[2 * P][:, :], AF.Silu)
                    ins.then_inc(sle, 1)

    def body_tensor(tensor):
        if deq:
            emit_t_deq(tensor)
        if gn:
            emit_t_gn(tensor)

    def emit_t_deq(tensor):
        tensor.wait_ge(zv, NCH_GU + 8)
        for g in range(32):
            for c in range(NCH_GU):
                gc = NCH_GU * g + c
                if gc >= 2:
                    tensor.wait_ge(dvc, gc - 1)
                tensor.matmul(gu_ps[2 * (gc % 2)][:, :], g32_sb[:, g * 128:(g + 1) * 128],
                              z_bf[:, c * 512:(c + 1) * 512], start=True, stop=True)
                ins = tensor.matmul(gu_ps[2 * (gc % 2) + 1][:, :], g32_sb[:, g * 128:(g + 1) * 128],
                                    s_bf[:, c * 512:(c + 1) * 512], start=True, stop=True)
                ins.then_inc(dtg, 1)
        for g in range(HKT):
            for q in range(8):
                gc = NCH_GU * 32 + 8 * g + q
                tensor.wait_ge(dvc, gc - 1)
                tensor.matmul(gu_ps[2 * (gc % 2)][:, :], g86_sb[:, g * 128:(g + 1) * 128],
                              zdn_bf[:, q * 512:(q + 1) * 512], start=True, stop=True)
                ins = tensor.matmul(gu_ps[2 * (gc % 2) + 1][:, :], g86_sb[:, g * 128:(g + 1) * 128],
                                    sdn_bf[:, q * 512:(q + 1) * 512], start=True, stop=True)
                ins.then_inc(dtg, 1)
    def emit_t_gn(tensor):
        for tc in range(NTC):
            tensor.wait_ge(lns, 2 * tc + 1)
            for k in range(KT):
                tensor.matmul(sum_ps[:, :], oc_sb[:, :], xlnb[:, k * TC:(k + 1) * TC],
                              start=(k == 0), stop=(k == KT - 1))
            for k in range(KT):
                ins = tensor.matmul(ssq_ps[:, :], oc_sb[:, :], sqv[:, k * TC:(k + 1) * TC],
                                    start=(k == 0), stop=(k == KT - 1))
            ins.then_inc(lnt, 1)
            tensor.wait_ge(lns, 2 * tc + 2)
            if tc > 0:
                tensor.wait_ge(de, NCH_DN * tc)
            tensor.matmul(mb_ps[:, :], or_sb[:, :], stats[:, 0 * TC:1 * TC], start=True, stop=True)
            ins = tensor.matmul(rb_ps[:, :], or_sb[:, :], stats[:, 1 * TC:2 * TC], start=True, stop=True)
            ins.then_inc(lnt, 1)
            tensor.wait_ge(lnv, 33 * tc + 33)
            for c in range(NCH_GU):
                i = tc * NCH_GU + c
                tensor.wait_ge(wg[i % 2], 16 * 4 * (i // 2 + 1))
                for P in range(2):
                    e = 2 * i + P
                    if e >= 2:
                        tensor.wait_ge(ge, e - 1)
                    for nt in (2 * P, 2 * P + 1):
                        p = gu_ps[nt]
                        for k in range(KT):
                            ins = tensor.matmul(p[:, :],
                                                wbuf[i % 2][:, k * 512 + nt * 128:k * 512 + nt * 128 + 128],
                                                xlnb[:, k * TC:(k + 1) * TC],
                                                start=(k == 0), stop=(k == KT - 1))
                    ins.then_inc(gm, 1)
            tensor.wait_ge(ge, 2 * NCH_GU * (tc + 1))
            for c in range(NCH_DN):
                j = tc * NCH_DN + c
                tensor.wait_ge(wd[j % 2], 16 * 4 * (j // 2 + 1))
                if j >= 2:
                    tensor.wait_ge(de, j - 1)
                p = cd_ps[c % 2]
                for k in range(HKT):
                    ins = tensor.matmul(p[:, :], dnw[c % 2][:, k * 128:(k + 1) * 128],
                                        hbuf[:, k * TC:(k + 1) * TC],
                                        start=(k == 0), stop=(k == HKT - 1))
                ins.then_inc(dm, 1)

    blk.gpsimd(body_gpsimd)
    blk.sync(body_sync)
    blk.vector(body_vector)
    blk.scalar(body_scalar)
    blk.tensor(body_tensor)
    return b, io



"""S5: L1 attention sublayer: xo = x + Proj(BlockAttn(Conv1d3(LN(x)))).

Halo tokens from the peer core arrive as xedge [D, 32] f32 (columns
[L_t, R_t] per owned tile t in order; zero columns at sequence boundaries).
Host layouts:
  cw_t [24, 128, 3, 32, 128] = conv_w.reshape(3,32,128,24,128).transpose(3,2,0,1,4)
  wq_t/wk_t/wv_t/wo_t [8, 128, 8, 128] = M.reshape(8,128,8,128).transpose(2,1,0,3)
  pw_t [32, 128, 8, 128] = proj_w.reshape(8,128,32,128).transpose(2,1,0,3)
"""
import numpy as np


W1 = 1024
CT = 24
NH = 16
DH1 = 64
NUSE = 88
NUNIT = 64


def build_s5(nc, name="s5", io=None):
    b = B(nc)
    if io is None:
        io = {}
        io["xr"] = nc.dram_tensor(name + "_xr", [D, TPC], F32, kind="ExternalInput").ap()
        io["xedge"] = nc.dram_tensor(name + "_xedge", [D, 32], F32, kind="ExternalInput").ap()
        for nm, shp in [("cw_t", [CT, 128, 3, KT, 128]), ("cb_pm", [128, CT]),
                        ("wq_t", [8, 128, 8, 128]), ("wk_t", [8, 128, 8, 128]),
                        ("wv_t", [8, 128, 8, 128]), ("wo_t", [8, 128, 8, 128]),
                        ("bq_pm", [128, 8]), ("bk_pm", [128, 8]), ("bv_pm", [128, 8]),
                        ("bo_pm", [128, 8]),
                        ("pw_t", [KT, 128, 8, 128]), ("pb_pm", [128, KT])]:
            io[nm] = nc.dram_tensor("l1a_" + nm, shp, F32, kind="ExternalInput").ap()
        io["xo"] = nc.dram_tensor(name + "_xo", [D, TPC], F32, kind="ExternalOutput").ap()
    (xr, xedge, cw_t, cb_pm, wq_t, wk_t, wv_t, wo_t, bq_pm, bk_pm, bv_pm, bo_pm,
     pw_t, pb_pm, xo) = (io[k] for k in
                         ["xr", "xedge", "cw_t", "cb_pm", "wq_t", "wk_t", "wv_t", "wo_t",
                          "bq_pm", "bk_pm", "bv_pm", "bo_pm", "pw_t", "pb_pm", "xo"])

    xeb = b.sb(name + "_xe", [128, KT * 520], BF16)
    cwb = [b.sb(f"{name}_cw{i}", [128, 3 * KT * 128], BF16) for i in range(2)]
    xconv = b.sb(name + "_xconv", [128, CT * TC], BF16)
    qkvh = b.sb(name + "_qkvh", [128, CT * TC], BF16)
    o_sb = b.sb(name + "_osb", [128, 8 * TC], BF16)
    y_sb = b.sb(name + "_ysb", [128, 8 * TC], BF16)
    wstr = [b.sb(f"{name}_ws{i}", [128, 8 * 128], BF16) for i in range(2)]
    eln = b.sb(name + "_eln", [128, KT * 32], BF16)
    xs2 = [b.sb(f"{name}_xs{i}", [128, TC], F32) for i in range(2)]
    xbt = [b.sb(f"{name}_xbt{i}", [128, TC], BF16) for i in range(2)]
    sqt = [b.sb(f"{name}_sqt{i}", [128, TC], BF16) for i in range(2)]
    xres = [b.sb(f"{name}_xr{i}", [128, TC], F32) for i in range(2)]
    xout = [b.sb(f"{name}_xo{i}", [128, TC], F32) for i in range(2)]
    tmps = [b.sb(f"{name}_tm{i}", [128, TC], F32) for i in range(2)]
    stats = b.sb(name + "_stats", [1, 3 * TC], F32)
    vtmp = b.sb(name + "_vtmp", [128, TC], F32)
    Sm_sb = b.sb(name + "_Sm", [128, 2 * 128], F32)
    P_sb = b.sb(name + "_P", [128, 2 * 128], BF16)
    PT_sb = b.sb(name + "_PT", [128, 2 * 128], BF16)
    VT_sb = b.sb(name + "_VT", [128, 2 * DH1], BF16)
    mrow = b.sb(name + "_mrow", [128, 4], F32)
    ident = b.sb(name + "_id", [128, 128], BF16)
    id64b = b.sb(name + "_id64", [128, 64], BF16)
    oc_sb = b.sb(name + "_ocs", [128, 1], BF16)
    or_sb = b.sb(name + "_ors", [1, 128], F32)
    cmask = b.sb(name + "_cmask", [128, 128], F32)
    bias_sb = b.sb(name + "_bias", [128, CT + 4 * 8 + KT], F32)

    onescol = b.const(name + "_onescol", np.ones((128, 1), np.float32))
    onesrow = b.const(name + "_onesrow", np.ones((1, 128), np.float32))
    identc = b.const(name + "_ident", np.eye(128, dtype=np.float32))
    id64b_a = np.zeros((128, 64), np.float32)
    for i_ in range(64):
        id64b_a[64 + i_, i_] = 1.0
    id64c = b.const(name + "_id64b", id64b_a)
    cmaskc = b.const(name + "_cmaskc",
                     np.where(np.tril(np.ones((128, 128))) > 0, 0.0, -8.0e9).astype(np.float32))

    bigA = b.ps(name + "_bigA", [128, TC])
    bigB = b.ps(name + "_bigB", [128, TC])
    big = [bigA, bigB]
    S_ps = [b.ps(f"{name}_S{i}", [128, 128]) for i in range(2)]
    T_ps = [b.ps(f"{name}_T{i}", [128, 128], BF16) for i in range(2)]
    O_ps = [b.ps(f"{name}_O{i}", [64, 128]) for i in range(2)]

    pre = b.sem(name + "_pre")
    cwq = [b.sem(name + "_cwq%d" % i) for i in range(2)]
    wsq = [b.sem(name + "_wsq%d" % i) for i in range(2)]
    lnq = [b.sem(name + "_lnq%d" % i) for i in range(2)]
    lns = b.sem(name + "_lns"); lnt = b.sem(name + "_lnt"); lnv = b.sem(name + "_lnv")
    tmm = b.sem(name + "_tmm")
    bev = b.sem(name + "_bev")
    am = b.sem(name + "_am"); av = b.sem(name + "_av"); asx = b.sem(name + "_asx")
    pva = b.sem(name + "_pva")
    oq = [b.sem(name + "_oq%d" % i) for i in range(2)]
    xq = [b.sem(name + "_xq%d" % i) for i in range(2)]

    NPRE = 11
    LNW = [32] + [TC] * NTC
    load_cons = []
    for q in range(5):
        for pss in range(2):
            for k in range(KT):
                load_cons.append(65 * q + (33 if pss else 0) + k + 1)

    def ln_src(q, k):
        if q == 0:
            return xedge.rearrange("(k p) t -> k p t", p=128)[k, :, :]
        cc = q - 1
        return xr.rearrange("(k p) t -> k p t", p=128)[k, :, cc * TC:(cc + 1) * TC]

    def widx_gidx(widx):
        cc, l = divmod(widx, 64)
        if l < 24:
            g = 24 + l
        elif l < 32:
            g = 48 + (l - 24)
        else:
            g = 56 + (l - 32)
        return NUSE * cc + g

    blk = b.es.enter_context(nc.Block())

    def body_gpsimd(g):
        g.dma_start(out=ident[:, :], in_=identc.ap()).then_inc(pre, 16)
        g.dma_start(out=id64b[:, :], in_=id64c.ap()).then_inc(pre, 16)
        g.dma_start(out=oc_sb[:, :], in_=onescol.ap()).then_inc(pre, 16)
        g.dma_start(out=or_sb[:, :], in_=onesrow.ap()).then_inc(pre, 16)
        g.dma_start(out=cmask[:, :], in_=cmaskc.ap()).then_inc(pre, 16)
        g.dma_start(out=bias_sb[:, 0:CT], in_=cb_pm).then_inc(pre, 16)
        g.dma_start(out=bias_sb[:, CT:CT + 8], in_=bq_pm).then_inc(pre, 16)
        g.dma_start(out=bias_sb[:, CT + 8:CT + 16], in_=bk_pm).then_inc(pre, 16)
        g.dma_start(out=bias_sb[:, CT + 16:CT + 24], in_=bv_pm).then_inc(pre, 16)
        g.dma_start(out=bias_sb[:, CT + 24:CT + 32], in_=bo_pm).then_inc(pre, 16)
        g.dma_start(out=bias_sb[:, CT + 32:CT + 32 + KT], in_=pb_pm).then_inc(pre, 16)
        for cc in range(NTC):
            for ct in range(CT):
                cidx = cc * CT + ct
                if cidx >= 2:
                    g.wait_ge(tmm, NUSE * ((cidx - 2) // CT) + ((cidx - 2) % CT) + 1)
                g.dma_start(out=cwb[cidx % 2][:, :], in_=cw_t[ct]).then_inc(cwq[cidx % 2], 16)
            for l in range(64):
                widx = cc * 64 + l
                if widx >= 2:
                    g.wait_ge(tmm, widx_gidx(widx - 2) + 1)
                if l < 8:
                    M = wq_t[l]
                elif l < 16:
                    M = wk_t[l - 8]
                elif l < 24:
                    M = wv_t[l - 16]
                elif l < 32:
                    M = wo_t[l - 24]
                else:
                    M = pw_t[l - 32]
                g.dma_start(out=wstr[widx % 2][:, :], in_=M).then_inc(wsq[widx % 2], 16)


    def emit_sync_ln(sync, q):
        for pss in range(2):
            for k in range(KT):
                gl = q * 64 + pss * KT + k
                if gl >= 2:
                    sync.wait_ge(lnv, load_cons[gl - 2])
                sync.dma_start(out=xs2[k % 2][:, 0:LNW[q]],
                               in_=ln_src(q, k)).then_inc(lnq[k % 2], 16)

    def body_sync(sync):
        for cc in range(NTC):
            if cc == 0:
                emit_sync_ln(sync, 0)
            emit_sync_ln(sync, cc + 1)
            for ot in range(KT):
                j = cc * KT + ot
                if j >= 2:
                    sync.wait_ge(pva, j - 1)
                sync.dma_start(out=xres[ot % 2][:, :],
                               in_=xr.rearrange("(k p) t -> k p t", p=128)[ot, :, cc * TC:(cc + 1) * TC]
                               ).then_inc(xq[j % 2], 16)
                if j >= 2:
                    jo = j - 2
                    cco, oto = divmod(jo, KT)
                    sync.dma_start(out=xo.rearrange("(k p) t -> k p t", p=128)[oto, :, cco * TC:(cco + 1) * TC],
                                   in_=xout[oto % 2][:, :]).then_inc(oq[jo % 2], 16)
        for jo in (NTC * KT - 2, NTC * KT - 1):
            cco, oto = divmod(jo, KT)
            sync.wait_ge(pva, jo + 1)
            sync.dma_start(out=xo.rearrange("(k p) t -> k p t", p=128)[oto, :, cco * TC:(cco + 1) * TC],
                           in_=xout[oto % 2][:, :]).then_inc(oq[jo % 2], 16)
        sync.wait_ge(oq[0], 16 * NTC * KT // 2)
        sync.wait_ge(oq[1], 16 * NTC * KT // 2)

    def emit_vec_ln(vector, q):
        A_ = ALU
        if True:
            W = LNW[q]
            for k in range(KT):
                gidx = q * 64 + k
                vector.wait_ge(lnq[k % 2], 16 * (gidx // 2 + 1))
                if k >= 2:
                    vector.wait_ge(lnt, 33 * q + k - 1)
                ins = vector.tensor_copy(xbt[k % 2][:, 0:W], xs2[k % 2][:, 0:W])
                ins.then_inc(lnv, 1)
            vector.wait_ge(lnt, 33 * q + 32)
            s_mean = stats[:, 0:W]
            s_var = stats[:, TC:TC + W]
            s_tmp = stats[:, 2 * TC:2 * TC + W]
            vector.tensor_scalar_mul(s_mean, bigA[0:1, 0:W], 1.0 / D)
            vector.tensor_scalar_mul(s_var, bigB[0:1, 0:W], 1.0 / D)
            vector.drain()
            vector.tensor_mul(s_tmp, s_mean, s_mean)
            vector.drain()
            vector.tensor_sub(s_var, s_var, s_tmp)
            vector.drain()
            ins = vector.tensor_scalar_add(s_var, s_var, 1e-5)
            ins.then_inc(lnv, 1)
            vector.wait_ge(lnt, 33 * q + 33)
            if q >= 2:
                vector.wait_ge(tmm, NUSE * (q - 2) + 24)
            for k in range(KT):
                gidx = q * 64 + 32 + k
                vector.wait_ge(lnq[k % 2], 16 * (gidx // 2 + 1))
                vector.tensor_sub(vtmp[:, 0:W], xs2[k % 2][:, 0:W], bigA[:, 0:W])
                vector.drain()
                if q == 0:
                    ins = vector.tensor_mul(eln[:, k * 32:(k + 1) * 32], vtmp[:, 0:32], bigB[:, 0:32])
                    ins.then_inc(lnv, 1)
                    vector.drain()
                else:
                    cc = q - 1
                    dst = xeb[:, k * 520:(k + 1) * 520].rearrange("p (t w) -> p t w", t=4)[:, :, 1:129]
                    vector.tensor_mul(dst, vtmp[:, 0:TC].rearrange("p (t w) -> p t w", t=4), bigB[:, 0:TC].rearrange("p (t w) -> p t w", t=4))
                    vector.drain()
                    hd = xeb[:, k * 520:(k + 1) * 520].rearrange("p (t w) -> p t w", t=4)[:, :, 0:130:129]
                    hs = eln[:, k * 32 + 8 * cc:k * 32 + 8 * cc + 8].rearrange("p (t s) -> p t s", t=4)
                    ins = vector.tensor_copy(hd, hs)
                    ins.then_inc(lnv, 1)
                    vector.drain()
    def body_vector(vector):
        A_ = ALU
        vector.wait_ge(pre, 16 * NPRE)
        for cc in range(NTC):
            if cc == 0:
                emit_vec_ln(vector, 0)
            emit_vec_ln(vector, cc + 1)
            for u0 in range(NUNIT):
                u = cc * NUNIT + u0
                tau, h = divmod(u0, NH)
                sl = u % 2
                vector.wait_ge(am, 4 * u + 1)
                if u >= 2:
                    vector.wait_ge(asx, u - 1)
                vector.tensor_add(Sm_sb[:, sl * 128:(sl + 1) * 128], S_ps[sl][:, :], cmask[:, :])
                vector.drain()
                vector.reduce_max(mrow[:, 2 * sl:2 * sl + 1], Sm_sb[:, sl * 128:(sl + 1) * 128], axis=AX.X)
                vector.drain()
                ins = vector.tensor_scalar_mul(mrow[:, 2 * sl + 1:2 * sl + 2], mrow[:, 2 * sl:2 * sl + 1], -0.125)
                ins.then_inc(av, 1)
                vector.drain()
                vector.wait_ge(asx, u + 1)
                vector.reduce_sum(mrow[:, 2 * sl:2 * sl + 1], P_sb[:, sl * 128:(sl + 1) * 128], axis=AX.X)
                vector.drain()
                vector.reciprocal(mrow[:, 2 * sl + 1:2 * sl + 2], mrow[:, 2 * sl:2 * sl + 1])
                vector.drain()
                ins = vector.tensor_scalar_mul(P_sb[:, sl * 128:(sl + 1) * 128],
                                               P_sb[:, sl * 128:(sl + 1) * 128], mrow[:, 2 * sl + 1:2 * sl + 2])
                ins.then_inc(av, 1)
                vector.drain()
                vector.wait_ge(am, 4 * u + 2)
                ins = vector.tensor_copy(PT_sb[:, sl * 128:(sl + 1) * 128], T_ps[sl][:, :])
                ins.then_inc(av, 1)
                vector.wait_ge(am, 4 * u + 3)
                ins = vector.tensor_copy(VT_sb[:, sl * DH1:(sl + 1) * DH1], T_ps[sl][:, 0:DH1])
                ins.then_inc(av, 1)
                vector.wait_ge(am, 4 * u + 4)
                ins = vector.tensor_copy(
                    o_sb[(h % 2) * 64:(h % 2) * 64 + 64, (h // 2) * TC + tau * 128:(h // 2) * TC + tau * 128 + 128],
                    O_ps[sl][:, :])
                ins.then_inc(av, 1)
            for ot in range(KT):
                j = cc * KT + ot
                vector.wait_ge(bev, NUSE * cc + 56 + ot + 1)
                vector.wait_ge(xq[j % 2], 16 * (j // 2 + 1))
                if j >= 2:
                    vector.wait_ge(oq[j % 2], 16 * (j // 2))
                ins = vector.tensor_add(xout[ot % 2][:, :], tmps[ot % 2][:, :], xres[ot % 2][:, :])
                ins.then_inc(pva, 1)

    def emit_scl_ln(scalar, q):
        if True:
            W = LNW[q]
            for k in range(KT):
                scalar.wait_ge(lnv, 65 * q + k + 1)
                if k >= 2:
                    scalar.wait_ge(lnt, 33 * q + k - 1)
                ins = scalar.activation(sqt[k % 2][:, 0:W], xbt[k % 2][:, 0:W], AF.Square)
                ins.then_inc(lns, 1)
            scalar.wait_ge(lnv, 65 * q + 33)
            s_var = stats[:, TC:TC + W]
            s_tmp = stats[:, 2 * TC:2 * TC + W]
            scalar.activation(s_tmp, s_var, AF.Ln)
            scalar.drain()
            ins = scalar.activation(s_var, s_tmp, AF.Exp, scale=-0.5)
            ins.then_inc(lns, 1)
    def body_scalar(scalar):
        for cc in range(NTC):
            if cc == 0:
                emit_scl_ln(scalar, 0)
            emit_scl_ln(scalar, cc + 1)
            for l in range(48):
                gidx = NUSE * cc + l
                scalar.wait_ge(tmm, gidx + 1)
                if l < 24:
                    dst = xconv[:, l * TC:(l + 1) * TC]
                    bcol = bias_sb[:, l:l + 1]
                else:
                    dst = qkvh[:, (l - 24) * TC:(l - 24 + 1) * TC]
                    bcol = bias_sb[:, CT + (l - 24):CT + (l - 24) + 1]
                ins = scalar.activation(dst, big[gidx % 2][:, :], AF.Identity, bias=bcol)
                ins.then_inc(bev, 1)
            for u0 in range(NUNIT):
                u = cc * NUNIT + u0
                sl = u % 2
                scalar.wait_ge(av, 5 * u + 1)
                ins = scalar.activation(P_sb[:, sl * 128:(sl + 1) * 128], Sm_sb[:, sl * 128:(sl + 1) * 128],
                                        AF.Exp, scale=0.125, bias=mrow[:, 2 * sl + 1:2 * sl + 2])
                ins.then_inc(asx, 1)
            for j in range(8):
                gidx = NUSE * cc + 48 + j
                scalar.wait_ge(tmm, gidx + 1)
                ins = scalar.activation(y_sb[:, j * TC:(j + 1) * TC], big[gidx % 2][:, :],
                                        AF.Identity, bias=bias_sb[:, CT + 24 + j:CT + 24 + j + 1])
                ins.then_inc(bev, 1)
            for ot in range(KT):
                gidx = NUSE * cc + 56 + ot
                j = cc * KT + ot
                scalar.wait_ge(tmm, gidx + 1)
                if j >= 2:
                    scalar.wait_ge(pva, j - 1)
                ins = scalar.activation(tmps[ot % 2][:, :], big[gidx % 2][:, :],
                                        AF.Identity, bias=bias_sb[:, CT + 32 + ot:CT + 32 + ot + 1])
                ins.then_inc(bev, 1)

    def body_tensor(tensor):
        tensor.wait_ge(pre, 16 * NPRE)
        for cc in range(NTC):
            qs = [0, cc + 1] if cc == 0 else [cc + 1]
            for q in qs:
                W = LNW[q]
                if q >= 1:
                    tensor.wait_ge(bev, NUSE * (q - 1))
                for k in range(KT):
                    tensor.wait_ge(lns, 33 * q + k + 1)
                    tensor.matmul(bigA[0:1, 0:W], oc_sb[:, :], xbt[k % 2][:, 0:W],
                                  start=(k == 0), stop=(k == KT - 1))
                    ins = tensor.matmul(bigB[0:1, 0:W], oc_sb[:, :], sqt[k % 2][:, 0:W],
                                        start=(k == 0), stop=(k == KT - 1))
                    ins.then_inc(lnt, 1)
                tensor.wait_ge(lns, 33 * q + 33)
                tensor.matmul(bigA[:, 0:W], or_sb[:, :], stats[:, 0:W], start=True, stop=True)
                ins = tensor.matmul(bigB[:, 0:W], or_sb[:, :], stats[:, TC:TC + W], start=True, stop=True)
                ins.then_inc(lnt, 1)
            tensor.wait_ge(lnv, 65 * (cc + 2))
            for ct in range(CT):
                cidx = cc * CT + ct
                gidx = NUSE * cc + ct
                tensor.wait_ge(cwq[cidx % 2], 16 * (cidx // 2 + 1))
                if gidx >= 2:
                    tensor.wait_ge(bev, gidx - 1)
                n = 0
                for tap in range(3):
                    for k in range(KT):
                        rhs = xeb[:, k * 520:(k + 1) * 520].rearrange("p (t w) -> p t w", t=4)[:, :, tap:tap + 128]
                        ins = tensor.matmul(big[gidx % 2][:, :],
                                            cwb[cidx % 2][:, (tap * KT + k) * 128:(tap * KT + k) * 128 + 128],
                                            rhs, start=(n == 0), stop=(n == 95))
                        n += 1
                ins.then_inc(tmm, 1)
            for l in range(24):
                widx = cc * 64 + l
                gidx = NUSE * cc + 24 + l
                tensor.wait_ge(wsq[widx % 2], 16 * (widx // 2 + 1))
                if l == 0:
                    tensor.wait_ge(bev, NUSE * cc + 24)
                tensor.wait_ge(bev, gidx - 1)
                m = l // 8
                for wk in range(8):
                    ins = tensor.matmul(big[gidx % 2][:, :], wstr[widx % 2][:, wk * 128:(wk + 1) * 128],
                                        xconv[:, (m * 8 + wk) * TC:(m * 8 + wk + 1) * TC],
                                        start=(wk == 0), stop=(wk == 7))
                ins.then_inc(tmm, 1)
            for u0 in range(NUNIT):
                u = cc * NUNIT + u0
                tau, h = divmod(u0, NH)
                sl = u % 2
                heT = h // 2
                rb = (h % 2) * 64
                if u0 == 0:
                    tensor.wait_ge(bev, NUSE * cc + 48)
                if u >= 2:
                    tensor.wait_ge(av, 5 * (u - 2) + 1)
                qsl = qkvh[rb:rb + 64, heT * TC + tau * 128:heT * TC + tau * 128 + 128]
                ksl = qkvh[rb:rb + 64, (8 + heT) * TC + tau * 128:(8 + heT) * TC + tau * 128 + 128]
                vsl = qkvh[rb:rb + 64, (16 + heT) * TC + tau * 128:(16 + heT) * TC + tau * 128 + 128]
                ins = tensor.matmul(S_ps[sl][:, :], qsl, ksl, start=True, stop=True)
                ins.then_inc(am, 1)
                tensor.wait_ge(av, 5 * u + 2)
                if u >= 2:
                    tensor.wait_ge(av, 5 * (u - 2) + 4)
                ins = tensor.transpose(T_ps[sl][:, :], P_sb[:, sl * 128:(sl + 1) * 128], ident[:, :])
                ins.then_inc(am, 1)
                tensor.wait_ge(av, 5 * u + 3)
                idsl = ident[0:64, 0:64] if rb == 0 else id64b[64:128, 0:64]
                ins = tensor.transpose(T_ps[sl][:, 0:64], vsl, idsl)
                ins.then_inc(am, 1)
                tensor.wait_ge(av, 5 * u + 4)
                if u >= 2:
                    tensor.wait_ge(av, 5 * (u - 2) + 5)
                ins = tensor.matmul(O_ps[sl][:, :], VT_sb[:, sl * DH1:(sl + 1) * DH1],
                                    PT_sb[:, sl * 128:(sl + 1) * 128], start=True, stop=True)
                ins.then_inc(am, 1)
            for jj in range(8):
                widx = cc * 64 + 24 + jj
                gidx = NUSE * cc + 48 + jj
                tensor.wait_ge(wsq[widx % 2], 16 * (widx // 2 + 1))
                if jj == 0:
                    tensor.wait_ge(av, 5 * NUNIT * (cc + 1))
                tensor.wait_ge(bev, gidx - 1)
                for kt in range(8):
                    ins = tensor.matmul(big[gidx % 2][:, :], wstr[widx % 2][:, kt * 128:(kt + 1) * 128],
                                        o_sb[:, kt * TC:(kt + 1) * TC],
                                        start=(kt == 0), stop=(kt == 7))
                ins.then_inc(tmm, 1)
            for ot in range(KT):
                widx = cc * 64 + 32 + ot
                gidx = NUSE * cc + 56 + ot
                tensor.wait_ge(wsq[widx % 2], 16 * (widx // 2 + 1))
                if ot == 0:
                    tensor.wait_ge(bev, NUSE * cc + 56)
                tensor.wait_ge(bev, gidx - 1)
                for kt in range(8):
                    ins = tensor.matmul(big[gidx % 2][:, :], wstr[widx % 2][:, kt * 128:(kt + 1) * 128],
                                        y_sb[:, kt * TC:(kt + 1) * TC],
                                        start=(kt == 0), stop=(kt == 7))
                ins.then_inc(tmm, 1)

    blk.gpsimd(body_gpsimd)
    blk.sync(body_sync)
    blk.vector(body_vector)
    blk.scalar(body_scalar)
    blk.tensor(body_tensor)
    return b, io


def host_s5(inp, prefix="L1_attn_"):
    cw = np.asarray(inp[prefix + "conv_w"], np.float32)
    out = {
        "l1a_cw_t": np.ascontiguousarray(cw.reshape(3, 32, 128, 24, 128).transpose(3, 2, 0, 1, 4)),
        "l1a_cb_pm": np.ascontiguousarray(np.asarray(inp[prefix + "conv_b"], np.float32).reshape(CT, 128).T),
        "l1a_bq_pm": np.ascontiguousarray(np.asarray(inp[prefix + "bq"], np.float32).reshape(-1).reshape(8, 128).T),
        "l1a_bk_pm": np.ascontiguousarray(np.asarray(inp[prefix + "bk"], np.float32).reshape(-1).reshape(8, 128).T),
        "l1a_bv_pm": np.ascontiguousarray(np.asarray(inp[prefix + "bv"], np.float32).reshape(-1).reshape(8, 128).T),
        "l1a_bo_pm": np.ascontiguousarray(np.asarray(inp[prefix + "bo"], np.float32).reshape(8, 128).T),
        "l1a_pb_pm": np.ascontiguousarray(np.asarray(inp[prefix + "proj_b"], np.float32).reshape(KT, 128).T),
    }
    for nm, key in [("wq_t", "wq"), ("wk_t", "wk"), ("wv_t", "wv")]:
        M = np.asarray(inp[prefix + key], np.float32).reshape(W1, W1)
        out["l1a_" + nm] = np.ascontiguousarray(M.reshape(8, 128, 8, 128).transpose(2, 1, 0, 3))
    M = np.asarray(inp[prefix + "wo"], np.float32).reshape(W1, W1)
    out["l1a_wo_t"] = np.ascontiguousarray(M.reshape(8, 128, 8, 128).transpose(2, 1, 0, 3))
    M = np.asarray(inp[prefix + "proj_w"], np.float32)
    out["l1a_pw_t"] = np.ascontiguousarray(M.reshape(8, 128, 32, 128).transpose(2, 1, 0, 3))
    return out

"""S2 (NSA sparse attention) split into two programs (S2a projections, S2b attention)."""
import numpy as np

import concourse.mybir as mybir

KVH = 8
G = 4
DH = 128
SCALE = float(DH) ** -0.5
A4 = SCALE / 4.0
FOLD = -30.0 / SCALE
NEG = -1.0e30


def build_s2a(nc, name="a2", io=None):
    b = B(nc)
    if io is None:
        io = {"xr": nc.dram_tensor(name + "_xr", [D, TPC], F32, kind="ExternalInput").ap()}
        for nm, shp in [("wqkv_t", [48, 128, 32 * 128]), ("wg_t", [128, 32 * 96]),
                        ("cost", [128, 16 * 64]), ("sint", [128, 16 * 64]),
                        ("kpos_pm", [128, 8 * 16]), ("vpos_pm", [128, 8 * 16]),
                        ("kw1_t", [16, 128, 16 * 128]), ("kw2_t", [1, 128, 16 * 128]),
                        ("vw1_t", [16, 128, 16 * 128]), ("vw2_t", [1, 128, 16 * 128]),
                        ("b1k_pm", [128, 16]), ("b2k_pm", [128, 1]),
                        ("b1v_pm", [128, 16]), ("b2v_pm", [128, 1]), ("bg_f", [96, 1])]:
            io[nm] = nc.dram_tensor("nsa_" + nm, shp, F32, kind="ExternalInput").ap()
        for nm, shp, dt in [("Qraw", [D, TPC], BF16), ("Qrope", [D, TPC], BF16),
                            ("gso", [96, TPC], F32), ("rko", [1024, TPC], BF16),
                            ("vto", [KVH, 128, TPC], BF16), ("cko", [128, KVH * 128], BF16),
                            ("cvo", [128, KVH * 128], BF16)]:
            io[nm] = nc.dram_tensor(name + "_" + nm, shp, dt, kind="ExternalOutput").ap()
    (xr, wqkv_t, wg_t, cost, sint, kpos_pm, vpos_pm, kw1_t, kw2_t, vw1_t, vw2_t,
     b1k_pm, b2k_pm, b1v_pm, b2v_pm, bg_f, Qraw, Qrope, gso, rko, vto, cko, cvo) = (
        io[k] for k in ["xr", "wqkv_t", "wg_t", "cost", "sint", "kpos_pm", "vpos_pm",
                        "kw1_t", "kw2_t", "vw1_t", "vw2_t", "b1k_pm", "b2k_pm",
                        "b1v_pm", "b2v_pm", "bg_f", "Qraw", "Qrope", "gso", "rko",
                        "vto", "cko", "cvo"])

    xlnb = b.sb(name + "_xln", [128, KT * TC], BF16)
    kown = b.sb(name + "_kown", [128, KVH * TPC], BF16)
    vown = b.sb(name + "_vown", [128, KVH * TPC], BF16)
    wstr = [b.sb(f"{name}_ws{i}", [128, KT * 128], BF16) for i in range(2)]
    sqv = b.sb(name + "_sqv", [128, KT * TC], BF16)
    wgb = b.sb(name + "_wgb", [128, KT * 96], BF16)
    csb = b.sb(name + "_csb", [128, 16 * 64], BF16)
    snb = b.sb(name + "_snb", [128, 16 * 64], BF16)
    kpv = b.sb(name + "_kpv", [128, 256], F32)
    bias2 = b.sb(name + "_b2", [128, 34], F32)
    bgs = b.sb(name + "_bgs", [96, 1], F32)
    xs2 = [b.sb(f"{name}_xs{i}", [128, TC], F32) for i in range(2)]
    stats = b.sb(name + "_stats", [1, 3 * TC], F32)
    vtmp = b.sb(name + "_vtmp", [128, TC], F32)
    stg = [b.sb(f"{name}_stg{i}", [128, TC], BF16) for i in range(2)]
    qb = [b.sb(f"{name}_qb{i}", [128, TC], BF16) for i in range(2)]
    rstg = [b.sb(f"{name}_rst{i}", [128, TC], BF16) for i in range(2)]
    gstg = [b.sb(f"{name}_gst{i}", [96, TC], F32) for i in range(2)]
    tkm = b.sb(name + "_tkm", [128, 128], BF16)
    rpd = b.sb(name + "_rpd", [128, 128], BF16)
    ckin = b.sb(name + "_ckin", [128, 16 * 128], BF16)
    chid = b.sb(name + "_chid", [128, 16 * 128], BF16)
    cst = [b.sb(f"{name}_cst{i}", [128, 128], BF16) for i in range(2)]
    oc_sb = b.sb(name + "_ocs", [128, 1], BF16)
    or_sb = b.sb(name + "_ors", [1, 128], F32)
    ident = b.sb(name + "_id", [128, 128], BF16)

    onescol = b.const(name + "_onescol", np.ones((128, 1), np.float32))
    onesrow = b.const(name + "_onesrow", np.ones((1, 128), np.float32))
    identc = b.const(name + "_ident", np.eye(128, dtype=np.float32))

    bigA = b.ps(name + "_bigA", [128, TC])
    bigB = b.ps(name + "_bigB", [128, TC])
    big = [bigA, bigB]
    T_ps = b.ps(name + "_Tp", [128, 128], BF16)
    C_ps = [b.ps(f"{name}_C{i}", [128, 128]) for i in range(2)]

    pre = b.sem(name + "_pre")
    lnc = b.sem(name + "_lnc")
    lnq = [b.sem(name + "_lnq%d" % i) for i in range(2)]
    lns = b.sem(name + "_lns"); lnt = b.sem(name + "_lnt"); lnv = b.sem(name + "_lnv")
    wsq = [b.sem(name + "_wsq%d" % i) for i in range(2)]
    tmm = b.sem(name + "_tmm")
    bev = b.sem(name + "_bev")
    rtA = b.sem(name + "_rtA")
    rvA = b.sem(name + "_rvA")
    qld = [b.sem(name + "_qld%d" % i) for i in range(2)]
    oqA = [b.sem(name + "_oqA%d" % i) for i in range(2)]
    oqB = [b.sem(name + "_oqB%d" % i) for i in range(2)]
    cvq = [b.sem(name + "_cvq%d" % i) for i in range(2)]
    cma = b.sem(name + "_cma")
    cmt = b.sem(name + "_cmt")
    cms = b.sem(name + "_cms")
    cmv = b.sem(name + "_cmv")
    oqG = [b.sem(name + "_oqG%d" % i) for i in range(2)]
    oqK = b.sem(name + "_oqK"); oqV = b.sem(name + "_oqV")
    oqR = b.sem(name + "_oqR"); oqT = b.sem(name + "_oqT")

    NPRE = 13
    NOT = 49
    NQU = 32 * NTC
    QR_T = 8 * NQU
    QR_V = 8 * NQU

    blk = b.es.enter_context(nc.Block())

    def body_gpsimd(g):
        g.dma_start(out=ident[:, :], in_=identc.ap()).then_inc(pre, 16)
        g.dma_start(out=oc_sb[:, :], in_=onescol.ap()).then_inc(pre, 16)
        g.dma_start(out=or_sb[:, :], in_=onesrow.ap()).then_inc(pre, 16)
        g.dma_start(out=csb[:, :], in_=cost).then_inc(pre, 16)
        g.dma_start(out=snb[:, :], in_=sint).then_inc(pre, 16)
        g.dma_start(out=kpv[:, 0:128], in_=kpos_pm).then_inc(pre, 16)
        g.dma_start(out=kpv[:, 128:256], in_=vpos_pm).then_inc(pre, 16)
        g.dma_start(out=bias2[:, 0:16], in_=b1k_pm).then_inc(pre, 16)
        g.dma_start(out=bias2[:, 16:32], in_=b1v_pm).then_inc(pre, 16)
        g.dma_start(out=bias2[:, 32:33], in_=b2k_pm).then_inc(pre, 16)
        g.dma_start(out=bias2[:, 33:34], in_=b2v_pm).then_inc(pre, 16)
        g.dma_start(out=bgs[:, :], in_=bg_f).then_inc(pre, 16)
        g.dma_start(out=wgb[:, :], in_=wg_t).then_inc(pre, 16)
        for cc in range(NTC):
            if cc > 0:
                g.wait_ge(tmm, NOT * cc)
            for k in range(KT):
                g.dma_start(out=xlnb[:, k * TC:(k + 1) * TC],
                            in_=xr.rearrange("(k p) t -> k p t", p=128)[k, :, cc * TC:(cc + 1) * TC]
                            ).then_inc(lnc, 16)
            for ot in range(48):
                widx = cc * 48 + ot
                if widx >= 2:
                    pr = widx - 2
                    g.wait_ge(tmm, NOT * (pr // 48) + (pr % 48) + 1)
                g.dma_start(out=wstr[widx % 2][:, :], in_=wqkv_t[ot]).then_inc(wsq[widx % 2], 16)
        for u in range(NQU):
            cc, ot = divmod(u, 32)
            g.wait_ge(oqA[u % 2], 16 * (u // 2 + 1))
            if u >= 2:
                g.wait_ge(rtA, 8 * (u - 1))
            g.dma_start(out=qb[u % 2][:, :],
                        in_=Qraw.rearrange("(o p) t -> o p t", p=128)[ot, :, cc * TC:(cc + 1) * TC]
                        ).then_inc(qld[u % 2], 16)
        for kv in range(KVH):
            for si in range(34):
                widx = kv * 34 + si
                if widx >= 2:
                    g.wait_ge(cmt, widx - 1)
                mat = (kw1_t[si] if si < 16 else
                       kw2_t[0] if si == 16 else
                       vw1_t[si - 17] if si < 33 else vw2_t[0])
                g.dma_start(out=wstr[widx % 2][:, 0:16 * 128], in_=mat).then_inc(cvq[widx % 2], 16)

    def body_sync(sync):
        for cc in range(NTC):
            for k in range(KT):
                li = cc * KT + k
                if li >= 2:
                    sync.wait_ge(lnv, 33 * cc + k - (1 if k < 2 else 0))
                sync.dma_start(out=xs2[k % 2][:, :],
                               in_=xr.rearrange("(k p) t -> k p t", p=128)[k, :, cc * TC:(cc + 1) * TC]
                               ).then_inc(lnq[k % 2], 16)
            for ot in range(NOT):
                gidx = NOT * cc + ot
                sync.wait_ge(bev, gidx + 1)
                if ot < 32:
                    sync.dma_start(out=Qraw.rearrange("(o p) t -> o p t", p=128)[ot, :, cc * TC:(cc + 1) * TC],
                                   in_=stg[ot % 2][:, :]).then_inc(oqA[ot % 2], 16)
                elif ot == 48:
                    sync.dma_start(out=gso[:, cc * TC:(cc + 1) * TC],
                                   in_=gstg[cc % 2][:, :]).then_inc(oqG[cc % 2], 16)
        for u in range(NQU):
            cc, ot = divmod(u, 32)
            sync.wait_ge(rvA, 8 * u + 8)
            sync.dma_start(out=Qrope.rearrange("(o p) t -> o p t", p=128)[ot, :, cc * TC:(cc + 1) * TC],
                           in_=rstg[u % 2][:, :]).then_inc(oqB[u % 2], 16)
        for kv in range(KVH):
            sync.wait_ge(cms, 34 * kv + 17)
            sync.dma_start(out=cko[:, kv * 128:(kv + 1) * 128],
                           in_=cst[0][:, :]).then_inc(oqK, 16)
            sync.wait_ge(cmv, kv + 1)
            sync.dma_start(out=cvo[:, kv * 128:(kv + 1) * 128],
                           in_=cst[1][:, :]).then_inc(oqV, 16)
            sync.wait_ge(rvA, QR_V + 48 * kv + 32)
            sync.dma_start(out=rko[kv * 128:(kv + 1) * 128, :],
                           in_=kown[:, kv * TPC:(kv + 1) * TPC]).then_inc(oqR, 16)
            sync.wait_ge(rvA, QR_V + 48 * kv + 48)
            sync.dma_start(out=vto[kv], in_=vown[:, kv * TPC:(kv + 1) * TPC]).then_inc(oqT, 16)
        for sm, tot in ((oqK, KVH), (oqV, KVH), (oqR, KVH), (oqT, KVH)):
            sync.wait_ge(sm, 16 * tot)
        for s in range(2):
            sync.wait_ge(oqG[s], 16 * NTC // 2)
            sync.wait_ge(oqA[s], 16 * NQU // 2)
            sync.wait_ge(oqB[s], 16 * NQU // 2)

    def emit_rope(vector, base_t, base_v, tdst, tdstslice, costile):
        A_ = ALU
        vector.wait_ge(rtA, base_t + 1)
        vector.tensor_copy(tkm[:, :], T_ps[:, :])
        vector.drain()
        co = csb[:, costile * 64:(costile + 1) * 64]
        si = snb[:, costile * 64:(costile + 1) * 64]
        x0 = tkm[:, 0:128:2]
        x1 = tkm[:, 1:128:2]
        t0 = vtmp[:, 0:64]
        t1 = vtmp[:, 64:128]
        vector.tensor_mul(t0, x0, co)
        vector.tensor_mul(t1, x1, si)
        vector.drain()
        vector.tensor_sub(rpd[:, 0:128:2], t0, t1)
        vector.drain()
        vector.tensor_mul(t0, x1, co)
        vector.tensor_mul(t1, x0, si)
        vector.drain()
        ins = vector.tensor_add(rpd[:, 1:128:2], t0, t1)
        ins.then_inc(rvA, 1)
        vector.drain()
        vector.wait_ge(rtA, base_t + 2)
        ins = vector.tensor_copy(tdst[:, tdstslice], T_ps[:, :])
        ins.then_inc(rvA, 1)
        vector.drain()

    def body_vector(vector):
        A_ = ALU
        vector.wait_ge(pre, 16 * NPRE)
        for cc in range(NTC):
            vector.wait_ge(lnt, 2 * cc + 1)
            s_mean = stats[:, 0:TC]
            s_var = stats[:, TC:2 * TC]
            s_tmp = stats[:, 2 * TC:3 * TC]
            vector.tensor_scalar_mul(s_mean, bigA[0:1, :], 1.0 / D)
            vector.tensor_scalar_mul(s_var, bigB[0:1, :], 1.0 / D)
            vector.drain()
            vector.tensor_mul(s_tmp, s_mean, s_mean)
            vector.drain()
            vector.tensor_sub(s_var, s_var, s_tmp)
            vector.drain()
            ins = vector.tensor_scalar_add(s_var, s_var, 1e-5)
            ins.then_inc(lnv, 1)
            vector.wait_ge(lnt, 2 * cc + 2)
            for k in range(KT):
                vector.wait_ge(lnq[k % 2], 16 * ((cc * KT + k) // 2 + 1))
                vector.tensor_sub(vtmp[:, :], xs2[k % 2][:, :], bigA[:, :])
                vector.drain()
                ins = vector.tensor_mul(xlnb[:, k * TC:(k + 1) * TC], vtmp[:, :], bigB[:, :])
                ins.then_inc(lnv, 1)
                vector.drain()
        for u in range(NQU):
            cc, ot = divmod(u, 32)
            for sub in range(4):
                m = 4 * u + sub
                if sub == 0 and u >= 2:
                    vector.wait_ge(oqB[u % 2], 16 * (u // 2))
                emit_rope(vector, 2 * m, 2 * m, rstg[u % 2], slice(sub * 128, (sub + 1) * 128),
                          4 * cc + sub)
        for kv in range(KVH):
            for br in range(2):
                src = kown if br == 0 else vown
                if kv == 0 and br == 0:
                    vector.wait_ge(bev, NOT * NTC)
                if br == 0 and kv > 0:
                    vector.wait_ge(rvA, QR_V + 48 * kv)
                for tw in range(16):
                    ins = vector.tensor_scalar(ckin[:, tw * 128:(tw + 1) * 128],
                                               src[:, kv * TPC + tw:kv * TPC + tw + 16 * 127 + 1:16],
                                               kpv[:, br * 128 + kv * 16 + tw:br * 128 + kv * 16 + tw + 1],
                                               None, A_.add)
                ins.then_inc(cma, 1)
                vector.drain()
                if br == 0:
                    vector.wait_ge(cmt, 34 * kv + 16)
            for t in range(16):
                bt = QR_T + 49 * kv + 2 * t
                emit_rope(vector, bt, bt, kown, slice(kv * TPC + t * 128, kv * TPC + (t + 1) * 128),
                          t)
            for t in range(16):
                bt = QR_T + 49 * kv + 32 + t
                vector.wait_ge(rtA, bt + 1)
                ins = vector.tensor_copy(vown[:, kv * TPC + t * 128:kv * TPC + (t + 1) * 128], T_ps[:, :])
                ins.then_inc(rvA, 1)
                vector.drain()
            vector.wait_ge(rtA, QR_T + 49 * kv + 49)
            ins = vector.tensor_copy(cst[1][:, :], T_ps[:, :])
            ins.then_inc(cmv, 1)
            vector.drain()

    def body_scalar(scalar):
        for cc in range(NTC):
            scalar.wait_ge(lnc, 16 * 32 * (cc + 1))
            for k in range(KT):
                ins = scalar.activation(sqv[:, k * TC:(k + 1) * TC],
                                        xlnb[:, k * TC:(k + 1) * TC], AF.Square)
            ins.then_inc(lns, 1)
            scalar.wait_ge(lnv, 33 * cc + 1)
            s_var = stats[:, TC:2 * TC]
            s_tmp = stats[:, 2 * TC:3 * TC]
            scalar.activation(s_tmp, s_var, AF.Ln)
            scalar.drain()
            ins = scalar.activation(s_var, s_tmp, AF.Exp, scale=-0.5)
            ins.then_inc(lns, 1)
            for ot in range(NOT):
                gidx = NOT * cc + ot
                scalar.wait_ge(tmm, gidx + 1)
                if ot < 32:
                    qsi = cc * 32 + ot
                    if qsi >= 2:
                        scalar.wait_ge(oqA[ot % 2], 16 * (qsi // 2))
                    ins = scalar.activation(stg[ot % 2][:, :], big[gidx % 2][:, :], AF.Copy)
                elif ot < 40:
                    kvh = ot - 32
                    ins = scalar.activation(kown[:, kvh * TPC + cc * TC:kvh * TPC + (cc + 1) * TC],
                                            big[gidx % 2][:, :], AF.Copy)
                elif ot < 48:
                    kvh = ot - 40
                    ins = scalar.activation(vown[:, kvh * TPC + cc * TC:kvh * TPC + (cc + 1) * TC],
                                            big[gidx % 2][:, :], AF.Copy)
                else:
                    if cc >= 2:
                        scalar.wait_ge(oqG[cc % 2], 16 * (cc // 2))
                    ins = scalar.activation(gstg[cc % 2][:, :], big[gidx % 2][0:96, :],
                                            AF.Sigmoid, bias=bgs[:, 0:1])
                ins.then_inc(bev, 1)
        for kv in range(KVH):
            for br in range(2):
                boff = 0 if br == 0 else 17
                for ot in range(16):
                    su = 34 * kv + br * 17 + ot
                    scalar.wait_ge(cmt, su + 1)
                    ins = scalar.activation(chid[:, ot * 128:(ot + 1) * 128], C_ps[ot % 2][:, :],
                                            AF.Relu, bias=bias2[:, 16 * br + ot:16 * br + ot + 1])
                    ins.then_inc(cms, 1)
                su = 34 * kv + br * 17 + 16
                scalar.wait_ge(cmt, su + 1)
                if br == 0 and kv >= 1:
                    scalar.wait_ge(oqK, 16 * kv)
                if br == 1 and kv >= 1:
                    scalar.wait_ge(oqV, 16 * kv)
                ins = scalar.activation(cst[0][:, :] if br == 0 else cst[1][:, :],
                                        C_ps[16 % 2][:, :], AF.Identity,
                                        bias=bias2[:, 32 + br:33 + br])
                ins.then_inc(cms, 1)

    def body_tensor(tensor):
        tensor.wait_ge(pre, 16 * NPRE)
        for cc in range(NTC):
            tensor.wait_ge(lns, 2 * cc + 1)
            if cc > 0:
                tensor.wait_ge(bev, NOT * cc)
            for k in range(KT):
                tensor.matmul(bigA[0:1, :], oc_sb[:, :], xlnb[:, k * TC:(k + 1) * TC],
                              start=(k == 0), stop=(k == KT - 1))
                ins = tensor.matmul(bigB[0:1, :], oc_sb[:, :], sqv[:, k * TC:(k + 1) * TC],
                                    start=(k == 0), stop=(k == KT - 1))
            ins.then_inc(lnt, 1)
            tensor.wait_ge(lns, 2 * cc + 2)
            tensor.matmul(bigA[:, :], or_sb[:, :], stats[:, 0:TC], start=True, stop=True)
            ins = tensor.matmul(bigB[:, :], or_sb[:, :], stats[:, TC:2 * TC], start=True, stop=True)
            ins.then_inc(lnt, 1)
            tensor.wait_ge(lnv, 33 * (cc + 1))
            for ot in range(NOT):
                gidx = NOT * cc + ot
                if gidx >= 2:
                    tensor.wait_ge(bev, gidx - 1)
                if ot < 48:
                    widx = cc * 48 + ot
                    tensor.wait_ge(wsq[widx % 2], 16 * (widx // 2 + 1))
                    for k in range(KT):
                        ins = tensor.matmul(big[gidx % 2][:, :],
                                            wstr[widx % 2][:, k * 128:(k + 1) * 128],
                                            xlnb[:, k * TC:(k + 1) * TC],
                                            start=(k == 0), stop=(k == KT - 1))
                else:
                    for k in range(KT):
                        ins = tensor.matmul(big[gidx % 2][0:96, :],
                                            wgb[:, k * 96:(k + 1) * 96],
                                            xlnb[:, k * TC:(k + 1) * TC],
                                            start=(k == 0), stop=(k == KT - 1))
                ins.then_inc(tmm, 1)
        for u in range(NQU):
            for sub in range(4):
                m = 4 * u + sub
                tensor.wait_ge(qld[u % 2], 16 * (u // 2 + 1))
                if m >= 1:
                    tensor.wait_ge(rvA, 2 * m)
                ins = tensor.transpose(T_ps[:, :], qb[u % 2][:, sub * 128:(sub + 1) * 128], ident[:, :])
                ins.then_inc(rtA, 1)
                tensor.wait_ge(rvA, 2 * m + 1)
                ins = tensor.transpose(T_ps[:, :], rpd[:, :], ident[:, :])
                ins.then_inc(rtA, 1)
        cbank_last = [0, 0]
        for kv in range(KVH):
            for br in range(2):
                tensor.wait_ge(cma, 2 * kv + br + 1)
                for ot in range(16):
                    widx = kv * 34 + br * 17 + ot
                    bank = ot % 2
                    tensor.wait_ge(cvq[widx % 2], 16 * (widx // 2 + 1))
                    if cbank_last[bank]:
                        tensor.wait_ge(cms, cbank_last[bank])
                    for u2 in range(16):
                        ins = tensor.matmul(C_ps[bank][:, :],
                                            wstr[widx % 2][:, u2 * 128:(u2 + 1) * 128],
                                            ckin[:, u2 * 128:(u2 + 1) * 128],
                                            start=(u2 == 0), stop=(u2 == 15))
                    ins.then_inc(cmt, 1)
                    cbank_last[bank] = widx + 1
                widx = kv * 34 + br * 17 + 16
                bank = 0
                tensor.wait_ge(cvq[widx % 2], 16 * (widx // 2 + 1))
                tensor.wait_ge(cms, 34 * kv + br * 17 + 16)
                if cbank_last[bank]:
                    tensor.wait_ge(cms, cbank_last[bank])
                for u2 in range(16):
                    ins = tensor.matmul(C_ps[bank][:, :],
                                        wstr[widx % 2][:, u2 * 128:(u2 + 1) * 128],
                                        chid[:, u2 * 128:(u2 + 1) * 128],
                                        start=(u2 == 0), stop=(u2 == 15))
                ins.then_inc(cmt, 1)
                cbank_last[bank] = widx + 1
            for t in range(16):
                bt = QR_T + 49 * kv + 2 * t
                m = (QR_V + 48 * kv + 2 * t)
                if m >= 1:
                    tensor.wait_ge(rvA, m)
                ins = tensor.transpose(T_ps[:, :], kown[:, kv * TPC + t * 128:kv * TPC + (t + 1) * 128],
                                       ident[:, :])
                ins.then_inc(rtA, 1)
                tensor.wait_ge(rvA, m + 1)
                ins = tensor.transpose(T_ps[:, :], rpd[:, :], ident[:, :])
                ins.then_inc(rtA, 1)
            for t in range(16):
                m = QR_V + 48 * kv + 32 + t
                tensor.wait_ge(rvA, m)
                ins = tensor.transpose(T_ps[:, :], vown[:, kv * TPC + t * 128:kv * TPC + (t + 1) * 128],
                                       ident[:, :])
                ins.then_inc(rtA, 1)
            tensor.wait_ge(cms, 34 * (kv + 1))
            tensor.wait_ge(rvA, QR_V + 48 * (kv + 1))
            if kv > 0:
                tensor.wait_ge(cmv, kv)
            ins = tensor.transpose(T_ps[:, :], cst[1][:, :], ident[:, :])
            ins.then_inc(rtA, 1)

    blk.gpsimd(body_gpsimd)
    blk.sync(body_sync)
    blk.vector(body_vector)
    blk.scalar(body_scalar)
    blk.tensor(body_tensor)
    return b, io


from concourse.bass import ds as _ds
LN1EM10 = -23.025850929940457


def build_s2b(nc, name="b2", io=None):
    b = B(nc)
    if io is None:
        io = {"xr": nc.dram_tensor(name + "_xr", [D, TPC], F32, kind="ExternalInput").ap()}
        for nm, shp, dt in [("Qraw", [D, TPC], BF16), ("Qrope", [D, TPC], BF16),
                            ("gso", [96, TPC], F32), ("rkf", [1024, 4160], BF16),
                            ("vtf_t", [KVH, 128, 33 * 128], BF16),
                            ("ckf", [128, KVH * 256], BF16), ("cvf", [256, 1024], BF16),
                            ("memk_pm", [128, KVH], F32), ("memv_row", [1, 1024], F32),
                            ("wo_t", [32, 128, 32 * 128], F32),
                            ("wkf", [NTC, 1024, 960], BF16),
                            ("wvf", [NTC, KVH, 128, 1024], BF16),
                            ("pos_pm", [128, 16], F32), ("own_pm", [128, 16], F32),
                            ("dmsel", [128, 256], F32), ("edsel", [128, 192], F32)]:
            io[nm] = nc.dram_tensor(name + "_" + nm, shp, dt, kind="ExternalInput").ap()
        io["xo"] = nc.dram_tensor(name + "_xo", [D, TPC], F32, kind="ExternalOutput").ap()
    (xr, Qraw, Qrope, gso, rkf, vtf_t, ckf, cvf, memk_pm, memv_row, wo_t,
     wkf, wvf, pos_pm, own_pm, dmsel, edsel, xo) = (
        io[k] for k in ["xr", "Qraw", "Qrope", "gso", "rkf", "vtf_t", "ckf", "cvf",
                        "memk_pm", "memv_row", "wo_t", "wkf", "wvf", "pos_pm", "own_pm",
                        "dmsel", "edsel", "xo"])

    rkb = [b.sb(f"{name}_rk{i}", [128, 4160], BF16) for i in range(2)]
    vtb = [b.sb(f"{name}_vt{i}", [128, 33 * 128], BF16) for i in range(2)]
    ckb = b.sb(name + "_ckb", [128, KVH * 257], BF16)
    ckb_v = ckb[:, :].rearrange("p (k w) -> p k w", k=KVH)
    cvtb = [b.sb(f"{name}_cvt{i}", [128, 256], BF16) for i in range(2)]
    memv_bf = b.sb(name + "_memv", [1, 1024], BF16)
    qr4 = [b.sb(f"{name}_qr4{i}", [128, 4 * TC], BF16) for i in range(2)]
    qp4 = [b.sb(f"{name}_qp4{i}", [128, 4 * TC], BF16) for i in range(2)]
    gch3 = b.sb(name + "_gch3", [1, 3 * TC], F32)
    impb = b.sb(name + "_imp", [128, 4 * 256], F32)
    bmTs = b.sb(name + "_bmT", [128, 2 * TC], BF16)
    bmxb = b.sb(name + "_bmxb", [128, 256], BF16)
    cmsk = b.sb(name + "_cmsk", [128, 4 * 257], F32)
    csx = b.sb(name + "_csx", [128, 2 * 257], F32)
    Pcb = b.sb(name + "_Pc", [128, 2 * 257], BF16)
    PcT = b.sb(name + "_PcT", [128, 2 * 128], BF16)
    pmem = b.sb(name + "_pmem", [1, 128], BF16)
    oc4 = b.sb(name + "_oc4", [128, 4 * TC], F32)
    ow1 = b.sb(name + "_ow1", [128, TC], F32)
    of1 = b.sb(name + "_of1", [128, TC], F32)
    dc4 = b.sb(name + "_dc4", [1, 4 * TC], F32)
    drw = b.sb(name + "_drw", [1, 2 * TC], F32)
    frw = b.sb(name + "_frw", [1, 3 * TC], F32)
    crow = b.sb(name + "_crow", [1, TC], F32)
    SWm = b.sb(name + "_SWm", [128, 192], F32)
    Pw = b.sb(name + "_Pw", [128, 2 * 192], BF16)
    PwTa = b.sb(name + "_PwTa", [128, 128], BF16)
    PwTb = b.sb(name + "_PwTb", [128, 128], BF16)
    bmx = b.sb(name + "_bmx", [128, 256], F32)
    bm01 = b.sb(name + "_bm01", [128, 256], F32)
    etmp = b.sb(name + "_etmp", [128, 257], F32)
    etm2 = b.sb(name + "_etm2", [128, 257], F32)
    etk = b.sb(name + "_etk", [128, 256], F32)
    trow = b.sb(name + "_trow", [128, 8], F32)
    ocmb = b.sb(name + "_ocmb", [128, 32 * TC], BF16)
    wstr = [b.sb(f"{name}_ws{i}", [128, 32 * 128], BF16) for i in range(2)]
    xres = [b.sb(f"{name}_xr{i}", [128, TC], F32) for i in range(2)]
    xout = [b.sb(f"{name}_xo{i}", [128, TC], F32) for i in range(2)]
    tmps = [b.sb(f"{name}_tm{i}", [128, TC], F32) for i in range(2)]
    cmb1 = b.sb(name + "_cmb1", [128, TC], F32)
    cmb2 = b.sb(name + "_cmb2", [128, TC], F32)
    pfT = [b.sb(f"{name}_pf{i}", [128, TC], BF16) for i in range(2)]
    wkslab = [b.sb(f"{name}_wksl{i}", [128, 960], BF16) for i in range(2)]
    wvslab = [b.sb(f"{name}_wvsl{i}", [128, 1024], BF16) for i in range(2)]
    posb = b.sb(name + "_posb", [128, 16], F32)
    ownb = b.sb(name + "_ownb", [128, 16], F32)
    mksb = b.sb(name + "_mksb", [128, KVH], F32)
    bc30 = b.sb(name + "_bc30", [128, 1], F32)
    bn30 = b.sb(name + "_bn30", [128, 1], F32)
    C16s = b.sb(name + "_C16", [128, 257], F32)
    jio = b.sb(name + "_jio", [128, 256], F32)
    dmk = b.sb(name + "_dmk", [128, 256], F32)
    bandk = b.sb(name + "_band", [128, 192], F32)
    edgek = b.sb(name + "_edge", [128, 192], F32)
    ETs = b.sb(name + "_ETs", [128, 2 * 2048], BF16)
    ocb = b.sb(name + "_ocb", [128, 1], BF16)
    orf = b.sb(name + "_orf", [1, 128], F32)
    identb = b.sb(name + "_idb", [128, 128], BF16)

    c16a = np.zeros((128, 257), np.float32)
    c16a[:, 0:256] = (16.0 * np.arange(256) + 15.0)[None, :]
    c16a[:, 256] = -1.0e9
    C16c = b.const(name + "_C16c", c16a)
    jioc = b.const(name + "_jioc",
                   np.broadcast_to(np.arange(256, dtype=np.float32), (128, 256)).copy())
    banda = np.full((128, 192), NEG, np.float32)
    for ii in range(128):
        for jj in range(192):
            if jj - 64 <= ii <= jj:
                banda[ii, jj] = 0.0
    bandc = b.const(name + "_bandc", banda)
    eta = np.zeros((2, 128, 2048), np.float32)
    for s in range(2):
        for j in range(2048):
            eta[s, (s * 2048 + j) // 16 - s * 128, j] = 1.0
    ETc = b.const(name + "_ETc", eta.transpose(1, 0, 2).reshape(128, 2 * 2048))
    ocolc = b.const(name + "_ocolc", np.ones((128, 1), np.float32))
    orowc = b.const(name + "_orowc", np.ones((1, 128), np.float32))
    identc = b.const(name + "_identc", np.eye(128, dtype=np.float32))
    bc30c = b.const(name + "_bc30c", np.full((128, 1), -30.0, np.float32))
    bn30c = b.const(name + "_bn30c", np.full((128, 1), -1.0e30, np.float32))

    Sf = [b.ps(f"{name}_Sf{i}", [128, TC]) for i in range(2)]
    OO = b.ps(name + "_OO", [128, TC])
    DDp = b.ps(name + "_DD", [1, TC])
    CSP = [b.ps(f"{name}_CS{i}", [128, 257]) for i in range(2)]
    TbB = b.ps(name + "_TbB", [128, 128], BF16)
    FB = b.ps(name + "_FB", [128, TC])

    names = ("pre qsy gql tcs vcs sxp ttr vpt tpv voc vt1 st1 vt2 st2 ttk sbk vtk "
             "twd vwd sxw twp vwp tpw vow tfq vda sfp tfv vof vfr tcb vcm vcb tmm sev pva").split()
    S = {n: b.sem(name + "_" + n) for n in names}
    wsq = [b.sem(name + "_wsq%d" % i) for i in range(2)]
    spre = b.sem(name + "_spre")
    oq = [b.sem(name + "_oq%d" % i) for i in range(2)]
    xq = [b.sem(name + "_xq%d" % i) for i in range(2)]

    NPRE = 16
    NJT = [8 * cc + 8 for cc in range(NTC)]

    def fbase(cc, kvh, g_):
        return sum(32 * NJT[c] for c in range(cc)) + (4 * kvh + g_) * NJT[cc]

    def fend(cc):
        return fbase(cc, 7, 3) + NJT[cc]

    def dab(cc, kvh, g_):
        return 8 * (32 * cc + 4 * kvh + g_)

    ET_ = mybir.EngineType
    blk = b.es.enter_context(nc.Block())

    def body_gpsimd(g):
        g.dma_start(out=C16s[:, :], in_=C16c.ap()).then_inc(S["pre"], 16)
        g.dma_start(out=jio[:, :], in_=jioc.ap()).then_inc(S["pre"], 16)
        g.dma_start(out=dmk[:, :], in_=dmsel).then_inc(S["pre"], 16)
        g.dma_start(out=bandk[:, :], in_=bandc.ap()).then_inc(S["pre"], 16)
        g.dma_start(out=edgek[:, :], in_=edsel).then_inc(S["pre"], 16)
        g.dma_start(out=ETs[:, :], in_=ETc.ap()).then_inc(S["pre"], 16)
        g.dma_start(out=ocb[:, :], in_=ocolc.ap()).then_inc(S["pre"], 16)
        g.dma_start(out=orf[:, :], in_=orowc.ap()).then_inc(S["pre"], 16)
        g.dma_start(out=identb[:, :], in_=identc.ap()).then_inc(S["pre"], 16)
        g.dma_start(out=posb[:, :], in_=pos_pm).then_inc(S["pre"], 16)
        g.dma_start(out=ownb[:, :], in_=own_pm).then_inc(S["pre"], 16)
        g.dma_start(out=ckb_v[:, :, 0:256],
                    in_=ckf.rearrange("p (k w) -> p k w", k=KVH)).then_inc(S["pre"], 16)
        g.dma_start(out=mksb[:, :], in_=memk_pm).then_inc(S["pre"], 16)
        g.dma_start(out=bc30[:, :], in_=bc30c.ap()).then_inc(S["pre"], 16)
        g.dma_start(out=bn30[:, :], in_=bn30c.ap()).then_inc(S["pre"], 16)
        g.dma_start(out=memv_bf[:, :], in_=memv_row).then_inc(S["pre"], 16)
        for cc in range(NTC):
            for ot in range(32):
                widx = cc * 32 + ot
                if widx >= 2:
                    g.wait_ge(S["tmm"], widx - 1)
                g.dma_start(out=wstr[widx % 2][:, :], in_=wo_t[ot]).then_inc(wsq[widx % 2], 16)

    def body_sync(sync):
        for cc in range(NTC):
            for kvh in range(KVH):
                u = 8 * cc + kvh
                sl = u % 2
                if u >= 2:
                    sync.wait_ge(S["vcb"], 4 * u - 4)
                sync.dma_start(out=rkb[sl][:, :],
                               in_=rkf[kvh * 128:(kvh + 1) * 128, :]).then_inc(S["qsy"], 16)
                sync.dma_start(out=vtb[sl][:, :], in_=vtf_t[kvh]).then_inc(S["qsy"], 16)
                sync.dma_start(out=cvtb[sl][:, 0:128],
                               in_=cvf[0:128, kvh * 128:(kvh + 1) * 128]).then_inc(S["qsy"], 16)
                sync.dma_start(out=cvtb[sl][:, 128:256],
                               in_=cvf[128:256, kvh * 128:(kvh + 1) * 128]).then_inc(S["qsy"], 16)
                sync.dma_start(out=qr4[sl][:, :].rearrange("p (k w) -> p k w", k=4),
                               in_=Qraw.rearrange("(h p) t -> p h t", p=128)
                               [:, 4 * kvh:4 * kvh + 4, cc * TC:(cc + 1) * TC]).then_inc(S["qsy"], 16)
                sync.dma_start(out=qp4[sl][:, :].rearrange("p (k w) -> p k w", k=4),
                               in_=Qrope.rearrange("(h p) t -> p h t", p=128)
                               [:, 4 * kvh:4 * kvh + 4, cc * TC:(cc + 1) * TC]).then_inc(S["qsy"], 16)
                sync.dma_start(out=wkslab[sl][:, :],
                               in_=wkf[cc, kvh * 128:(kvh + 1) * 128, :]).then_inc(S["qsy"], 16)
                sync.dma_start(out=wvslab[sl][:, :], in_=wvf[cc, kvh]).then_inc(S["qsy"], 16)
                for g_ in range(4):
                    h = 4 * kvh + g_
                    if 4 * u + g_ >= 1:
                        sync.wait_ge(S["vfr"], 4 * u + g_)
                    sync.dma_start(out=gch3[0:1, :].rearrange("p (k w) -> p k w", k=3),
                                   in_=gso[3 * h:3 * h + 3, cc * TC:(cc + 1) * TC]
                                   ).then_inc(S["gql"], 16)
            for ot in range(KT):
                j = cc * KT + ot
                if j >= 2:
                    sync.wait_ge(S["pva"], j - 1)
                sync.dma_start(out=xres[ot % 2][:, :],
                               in_=xr.rearrange("(k p) t -> k p t", p=128)[ot, :, cc * TC:(cc + 1) * TC]
                               ).then_inc(xq[j % 2], 16)
                if j >= 2:
                    jo = j - 2
                    cco, oto = divmod(jo, KT)
                    sync.dma_start(out=xo.rearrange("(k p) t -> k p t", p=128)
                                   [oto, :, cco * TC:(cco + 1) * TC],
                                   in_=xout[oto % 2][:, :]).then_inc(oq[jo % 2], 16)
        for jo in (NTC * KT - 2, NTC * KT - 1):
            cco, oto = divmod(jo, KT)
            sync.wait_ge(S["pva"], jo + 1)
            sync.dma_start(out=xo.rearrange("(k p) t -> k p t", p=128)
                           [oto, :, cco * TC:(cco + 1) * TC],
                           in_=xout[oto % 2][:, :]).then_inc(oq[jo % 2], 16)
        sync.wait_ge(oq[0], 16 * (NTC * KT // 2))
        sync.wait_ge(oq[1], 16 * (NTC * KT // 2))

    def body_tensor(tensor):
        tensor.wait_ge(S["pre"], 16 * NPRE)
        tensor.wait_ge(spre, 1)
        for cc in range(NTC):
            for kvh in range(KVH):
                u = 8 * cc + kvh
                sl = u % 2
                tensor.wait_ge(S["qsy"], 128 * (u + 1))

                def comp_tail(idx_l):
                    g_, it = divmod(idx_l, 4)
                    idx = 16 * u + idx_l
                    tensor.wait_ge(S["sxp"], idx + 1)
                    for s in range(3):
                        tensor.wait_ge(S["vpt"], 3 * idx + s)
                        if s < 2:
                            ins = tensor.transpose(
                                TbB[:, :], Pcb[:, (idx % 2) * 257 + s * 128:(idx % 2) * 257 + (s + 1) * 128],
                                identb[:, :])
                        else:
                            ins = tensor.transpose(
                                TbB[0:1, :], Pcb[:, (idx % 2) * 257 + 256:(idx % 2) * 257 + 257],
                                identb[:, :])
                        ins.then_inc(S["ttr"], 1)
                    tensor.wait_ge(S["vpt"], 3 * (idx + 1))
                    if it == 0:
                        if g_ == 0:
                            if u > 0:
                                tensor.wait_ge(S["vof"], 4 * u)
                        else:
                            tensor.wait_ge(S["voc"], 4 * u + g_)
                    ocol = OO[:, it * 128:(it + 1) * 128]
                    tensor.matmul(ocol, cvtb[sl][:, 0:128], PcT[:, 0:128],
                                  start=(it == 0), stop=False)
                    tensor.matmul(ocol, cvtb[sl][:, 128:256], PcT[:, 128:256],
                                  start=False, stop=False)
                    tensor.matmul(ocol, memv_bf[0:1, kvh * 128:(kvh + 1) * 128], pmem[0:1, :],
                                  start=False, stop=(it == 3))
                    dcol = DDp[0:1, it * 128:(it + 1) * 128]
                    tensor.matmul(dcol, ocb[:, 0:1], PcT[:, 0:128],
                                  start=(it == 0), stop=False)
                    tensor.matmul(dcol, ocb[:, 0:1], PcT[:, 128:256], start=False, stop=False)
                    tensor.matmul(dcol, ocb[0:1, 0:1], pmem[0:1, :],
                                  start=False, stop=(it == 3)).then_inc(S["tpv"], 1)

                for idx_l in range(17):
                    if idx_l < 16:
                        g_, it = divmod(idx_l, 4)
                        idx = 16 * u + idx_l
                        if idx >= 2:
                            tensor.wait_ge(S["vcs"], idx - 1)
                        if u >= 1 and idx_l < 2:
                            tensor.wait_ge(S["vwd"], 16 * u)
                        tensor.matmul(CSP[idx % 2][:, :],
                                      qr4[sl][:, g_ * TC + it * 128:g_ * TC + (it + 1) * 128],
                                      ckb[:, kvh * 257:(kvh + 1) * 257],
                                      start=True, stop=True).then_inc(S["tcs"], 1)
                    if idx_l > 0:
                        comp_tail(idx_l - 1)

                for it in range(4):
                    tdx = 4 * u + it
                    tensor.wait_ge(S["vtk"], tdx + 1)
                    for s in range(2):
                        tensor.wait_ge(S["sbk"], 2 * tdx + s)
                        if it == 0 and s == 0:
                            tensor.wait_ge(S["vpt"], 3 * (16 * u + 16))
                        tensor.transpose(TbB[:, :], bmxb[:, s * 128:(s + 1) * 128],
                                         identb[:, :]).then_inc(S["ttk"], 1)

                for g_ in range(4):
                    fb = fbase(cc, kvh, g_)
                    NJ = NJT[cc]
                    for it in range(4):
                        wdx = 16 * u + 4 * g_ + it
                        if g_ == 0 and it < 2:
                            tensor.wait_ge(S["vcs"], 16 * u + 16)
                        else:
                            tensor.wait_ge(S["vwd"], wdx - 1)
                        tensor.matmul(CSP[wdx % 2][:, 0:192],
                                      qp4[sl][:, g_ * TC + it * 128:g_ * TC + (it + 1) * 128],
                                      wkslab[sl][:, 2 * it * 128:2 * it * 128 + 192],
                                      start=True, stop=True).then_inc(S["twd"], 1)
                        tensor.wait_ge(S["sxw"], wdx + 1)
                        for s in range(2):
                            tensor.wait_ge(S["vwp"], 2 * wdx + s)
                            if g_ == 0 and it == 0 and s == 0:
                                tensor.wait_ge(S["sbk"], 8 * (u + 1))
                            if s == 0:
                                ins = tensor.transpose(TbB[0:64, :],
                                                       Pw[:, (wdx % 2) * 192:(wdx % 2) * 192 + 64],
                                                       identb[:, :])
                            else:
                                ins = tensor.transpose(TbB[:, :],
                                                       Pw[:, (wdx % 2) * 192 + 64:(wdx % 2) * 192 + 192],
                                                       identb[:, :])
                            ins.then_inc(S["twp"], 1)
                        tensor.wait_ge(S["vwp"], 2 * (wdx + 1))
                        if it == 0:
                            if g_ == 0:
                                tensor.wait_ge(S["voc"], 4 * u + 4)
                            else:
                                tensor.wait_ge(S["vof"], 4 * u + g_)
                        ocol = OO[:, it * 128:(it + 1) * 128]
                        tensor.matmul(ocol, wvslab[sl][64:128, 2 * it * 128:2 * it * 128 + 128],
                                      PwTa[64:128, :], start=(it == 0), stop=False)
                        tensor.matmul(ocol, wvslab[sl][:, (2 * it + 1) * 128:(2 * it + 2) * 128],
                                      PwTb[:, :], start=False, stop=(it == 3))
                        dcol = DDp[0:1, it * 128:(it + 1) * 128]
                        tensor.matmul(dcol, ocb[64:128, 0:1], PwTa[64:128, :],
                                      start=(it == 0), stop=False)
                        tensor.matmul(dcol, ocb[:, 0:1], PwTb[:, :],
                                      start=False, stop=(it == 3)).then_inc(S["tpw"], 1)
                    def fine_pv(jt):
                        jdx = fb + jt
                        tensor.wait_ge(S["sfp"], jdx + 1)
                        if jt == 0:
                            tensor.wait_ge(S["vow"], 4 * u + g_ + 1)
                        st = (jt == 0)
                        sp = (jt == NJ - 1)
                        tensor.matmul(OO[:, :], vtb[sl][:, (jt + 1) * 128:(jt + 2) * 128],
                                      pfT[jt % 2][:, :], start=st, stop=sp)
                        tensor.matmul(DDp[0:1, :], ocb[:, 0:1], pfT[jt % 2][:, :],
                                      start=st, stop=sp).then_inc(S["tfv"], 1)

                    for jt in range(NJ):
                        jdx = fb + jt
                        if jdx >= 2:
                            tensor.wait_ge(S["sfp"], jdx - 1)
                        if cc >= 1 and kvh == 0 and g_ == 0 and jt < 2:
                            tensor.wait_ge(S["sev"], 32 * cc)
                        tensor.matmul(Sf[jt % 2][:, :], rkb[sl][:, 64 + jt * 128:64 + (jt + 1) * 128],
                                      qp4[sl][:, g_ * TC:(g_ + 1) * TC], start=True, stop=False)
                        sh = jt // 16
                        tensor.matmul(Sf[jt % 2][:, :],
                                      ETs[:, sh * 2048 + (jt % 16) * 128:sh * 2048 + (jt % 16 + 1) * 128],
                                      bmTs[:, sh * TC:(sh + 1) * TC],
                                      start=False, stop=True).then_inc(S["tfq"], 1)
                        if jt > 0:
                            fine_pv(jt - 1)
                    fine_pv(NJ - 1)
                    for br in range(3):
                        cdx = 3 * (4 * u + g_) + br
                        tensor.wait_ge(S["vfr"], 4 * u + g_ + 1)
                        if cdx >= 1:
                            tensor.wait_ge(S["vcm"], cdx)
                        tensor.matmul(FB[:, :], orf[0:1, :], frw[0:1, br * TC:(br + 1) * TC],
                                      start=True, stop=True).then_inc(S["tcb"], 1)

            tensor.wait_ge(S["vcb"], 32 * (cc + 1))
            tensor.wait_ge(S["sfp"], fend(cc))
            for ot in range(32):
                widx = cc * 32 + ot
                tensor.wait_ge(wsq[widx % 2], 16 * (widx // 2 + 1))
                if widx >= 2:
                    tensor.wait_ge(S["sev"], widx - 1)
                for kt in range(32):
                    m = tensor.matmul(Sf[ot % 2][:, :], wstr[widx % 2][:, kt * 128:(kt + 1) * 128],
                                      ocmb[:, kt * TC:(kt + 1) * TC],
                                      start=(kt == 0), stop=(kt == 31))
                m.then_inc(S["tmm"], 1)

    def body_scalar(scalar):
        scalar.wait_ge(S["pre"], 16 * NPRE)
        for kv in range(KVH):
            ins = scalar.activation(ckb_v[:, kv, 256:257], mksb[:, kv:kv + 1], AF.Copy)
        ins.then_inc(spre, 1)
        for cc in range(NTC):
            for kvh in range(KVH):
                u = 8 * cc + kvh
                for idx_l in range(16):
                    idx = 16 * u + idx_l
                    scalar.wait_ge(S["vcs"], idx + 1)
                    if idx >= 2:
                        scalar.wait_ge(S["ttr"], 3 * (idx - 1))
                    scalar.activation(Pcb[:, (idx % 2) * 257:(idx % 2 + 1) * 257],
                                      csx[:, (idx % 2) * 257:(idx % 2 + 1) * 257],
                                      AF.Exp, bias=bc30[:, 0:1], scale=SCALE).then_inc(S["sxp"], 1)
                for it in range(4):
                    tdx = 4 * u + it
                    scalar.wait_ge(S["vt1"], tdx + 1)
                    scalar.activation(etk[:, :], impb[:, it * 256:(it + 1) * 256],
                                      AF.Exp, bias=trow[:, 2:3], scale=A4).then_inc(S["st1"], 1)
                    scalar.activation(trow[:, 4:5], trow[:, 3:4], AF.Exp).then_inc(S["st1"], 1)
                    scalar.wait_ge(S["vt2"], tdx + 1)
                    scalar.activation(trow[:, 7:8], trow[:, 6:7], AF.Ln).then_inc(S["st2"], 1)
                    for s in range(2):
                        scalar.wait_ge(S["ttk"], 2 * tdx + s + 1)
                        scalar.activation(bmTs[:, s * TC + it * 128:s * TC + (it + 1) * 128],
                                          TbB[:, :], AF.Identity,
                                          bias=bn30[:, 0:1], scale=1.0e30).then_inc(S["sbk"], 1)
                for g_ in range(4):
                    fb = fbase(cc, kvh, g_)
                    NJ = NJT[cc]
                    for it in range(4):
                        wdx = 16 * u + 4 * g_ + it
                        scalar.wait_ge(S["vwd"], wdx + 1)
                        if wdx >= 2:
                            scalar.wait_ge(S["twp"], 2 * (wdx - 1))
                        scalar.activation(Pw[:, (wdx % 2) * 192:(wdx % 2 + 1) * 192],
                                          SWm[:, :], AF.Exp,
                                          bias=bc30[:, 0:1], scale=SCALE).then_inc(S["sxw"], 1)
                    for jt in range(NJ):
                        jdx = fb + jt
                        scalar.wait_ge(S["tfq"], jdx + 1)
                        if jt >= 8 * cc:
                            scalar.wait_ge(S["vda"], dab(cc, kvh, g_) + (jt - 8 * cc) + 1)
                        if jdx >= 2:
                            scalar.wait_ge(S["tfv"], jdx - 1)
                        scalar.activation(pfT[jt % 2][:, :], Sf[jt % 2][:, :], AF.Exp,
                                          bias=bc30[:, 0:1], scale=SCALE).then_inc(S["sfp"], 1)
            for ot in range(32):
                widx = cc * 32 + ot
                scalar.wait_ge(S["tmm"], widx + 1)
                if widx >= 2:
                    scalar.wait_ge(S["pva"], widx - 1)
                scalar.activation(tmps[ot % 2][:, :], Sf[ot % 2][:, :],
                                  AF.Copy).then_inc(S["sev"], 1)

    def body_vector(vector):
        vector.wait_ge(S["pre"], 16 * NPRE)
        for cc in range(NTC):
            for it in range(4):
                vector.tensor_scalar(etmp[:, 0:257], C16s[:, :],
                                     posb[:, 4 * cc + it:4 * cc + it + 1], None, ALU.is_lt)
                vector.drain()
                vector.tensor_scalar(cmsk[:, it * 257:(it + 1) * 257], etmp[:, 0:257],
                                     1.0e30, -1.0e30, ALU.mult, ALU.add)
                vector.drain()
            for kvh in range(KVH):
                u = 8 * cc + kvh

                def pc_evac(idx_l):
                    g_, it = divmod(idx_l, 4)
                    idx = 16 * u + idx_l
                    for s in range(3):
                        vector.wait_ge(S["ttr"], 3 * idx + s + 1)
                        if s < 2:
                            ins = vector.tensor_copy(PcT[:, s * 128:(s + 1) * 128], TbB[:, :])
                        else:
                            ins = vector.tensor_copy(pmem[0:1, :], TbB[0:1, :])
                        ins.then_inc(S["vpt"], 1)
                    if it == 3:
                        vector.wait_ge(S["tpv"], 16 * u + 4 * g_ + 4)
                        vector.tensor_copy(oc4[:, g_ * TC:(g_ + 1) * TC], OO[:, :])
                        vector.tensor_copy(dc4[0:1, g_ * TC:(g_ + 1) * TC],
                                           DDp[0:1, :]).then_inc(S["voc"], 1)
                        vector.drain()

                for idx_l in range(16):
                    idx = 16 * u + idx_l
                    g_, it = divmod(idx_l, 4)
                    vector.wait_ge(S["tcs"], idx + 1)
                    vector.tensor_tensor(csx[:, (idx % 2) * 257:(idx % 2 + 1) * 257],
                                         CSP[idx % 2][:, :], cmsk[:, it * 257:(it + 1) * 257],
                                         ALU.add)
                    vector.drain()
                    if g_ == 0:
                        ins = vector.tensor_copy(impb[:, it * 256:(it + 1) * 256],
                                                 csx[:, (idx % 2) * 257:(idx % 2) * 257 + 256])
                    else:
                        ins = vector.tensor_tensor(impb[:, it * 256:(it + 1) * 256],
                                                   impb[:, it * 256:(it + 1) * 256],
                                                   csx[:, (idx % 2) * 257:(idx % 2) * 257 + 256],
                                                   ALU.add)
                    ins.then_inc(S["vcs"], 1)
                    vector.drain()
                    if idx_l > 0:
                        pc_evac(idx_l - 1)
                pc_evac(15)

                for it in range(4):
                    tdx = 4 * u + it
                    vector.wait_ge(S["vcs"], 16 * u + 13 + it)
                    vector.reduce_max(trow[:, 0:1], impb[:, it * 256:(it + 1) * 256], axis=AX.X)
                    vector.drain()
                    vector.tensor_scalar(trow[:, 1:2], trow[:, 0:1], A4, -1000.0,
                                         ALU.mult, ALU.max)
                    vector.drain()
                    vector.tensor_scalar(trow[:, 2:3], trow[:, 1:2], -1.0, None, ALU.mult)
                    vector.tensor_scalar(trow[:, 3:4], trow[:, 1:2], -1.0, -1000.0,
                                         ALU.mult, ALU.add).then_inc(S["vt1"], 1)
                    vector.drain()
                    vector.wait_ge(S["st1"], 2 * tdx + 2)
                    vector.reduce_sum(trow[:, 5:6], etk[:, :], axis=AX.X)
                    vector.drain()
                    vector.tensor_tensor(trow[:, 6:7], trow[:, 5:6], trow[:, 4:5],
                                         ALU.add).then_inc(S["vt2"], 1)
                    vector.drain()
                    vector.wait_ge(S["st2"], tdx + 1)
                    vector.tensor_tensor(trow[:, 5:6], trow[:, 1:2], trow[:, 7:8], ALU.add)
                    vector.drain()
                    vector.tensor_scalar(trow[:, 6:7], trow[:, 5:6], 1.0 / A4, LN1EM10 / A4,
                                         ALU.mult, ALU.add)
                    vector.tensor_scalar(bmx[:, :], impb[:, it * 256:(it + 1) * 256],
                                         1.0, None, ALU.mult)
                    vector.tensor_scalar(bm01[:, :], impb[:, it * 256:(it + 1) * 256],
                                         0.0, None, ALU.mult)
                    vector.drain()
                    for r_ in range(4):
                        vector.reduce_max(trow[:, 0:1], bmx[:, :], axis=AX.X)
                        vector.drain()
                        vector.tensor_tensor(trow[:, 1:2], trow[:, 0:1], trow[:, 6:7], ALU.is_gt)
                        vector.tensor_scalar(etmp[:, 0:256], bmx[:, :], trow[:, 0:1],
                                             None, ALU.is_equal)
                        vector.drain()
                        vector.tensor_scalar(etm2[:, 0:256], etmp[:, 0:256], trow[:, 1:2],
                                             None, ALU.mult)
                        vector.drain()
                        vector.tensor_tensor(bm01[:, :], bm01[:, :], etm2[:, 0:256], ALU.max)
                        vector.tensor_scalar(etmp[:, 0:256], etm2[:, 0:256], -1.0e32,
                                             None, ALU.mult)
                        vector.drain()
                        vector.tensor_tensor(bmx[:, :], bmx[:, :], etmp[:, 0:256], ALU.add)
                        vector.drain()
                    vector.tensor_scalar(etmp[:, 0:256], jio[:, :],
                                         ownb[:, 4 * cc + it:4 * cc + it + 1], None, ALU.is_equal)
                    vector.drain()
                    vector.tensor_tensor(bm01[:, :], bm01[:, :], etmp[:, 0:256], ALU.max)
                    vector.drain()
                    if tdx >= 1:
                        vector.wait_ge(S["ttk"], 2 * tdx)
                    vector.tensor_copy(bmxb[:, :], bm01[:, :]).then_inc(S["vtk"], 1)
                    vector.drain()

                for g_ in range(4):
                    fb = fbase(cc, kvh, g_)
                    NJ = NJT[cc]
                    for it in range(4):
                        wdx = 16 * u + 4 * g_ + it
                        vector.wait_ge(S["twd"], wdx + 1)
                        if wdx >= 1:
                            vector.wait_ge(S["sxw"], wdx)
                        ins = vector.tensor_tensor(SWm[:, :], CSP[wdx % 2][:, 0:192],
                                                   bandk[:, :], ALU.add)
                        if cc == 0 and it == 0:
                            vector.drain()
                            ins = vector.tensor_tensor(SWm[:, :], SWm[:, :],
                                                       edgek[:, :], ALU.add)
                        ins.then_inc(S["vwd"], 1)
                        for s in range(2):
                            vector.wait_ge(S["twp"], 2 * wdx + s + 1)
                            if s == 0:
                                if wdx >= 1:
                                    vector.wait_ge(S["tpw"], wdx)
                                ins = vector.tensor_copy(PwTa[64:128, :], TbB[0:64, :])
                            else:
                                ins = vector.tensor_copy(PwTb[:, :], TbB[:, :])
                            ins.then_inc(S["vwp"], 1)
                    vector.wait_ge(S["tpw"], 16 * u + 4 * g_ + 4)
                    vector.tensor_copy(ow1[:, :], OO[:, :])
                    vector.tensor_copy(drw[0:1, 0:TC], DDp[0:1, :]).then_inc(S["vow"], 1)
                    vector.drain()
                    for kk in range(8):
                        it_, d_ = divmod(kk, 2)
                        jt = 8 * cc + 2 * it_ + d_
                        jdx = fb + jt
                        vector.wait_ge(S["tfq"], jdx + 1)
                        if jdx >= 1:
                            vector.wait_ge(S["sfp"], jdx - 1)
                        vector.tensor_tensor(Sf[d_][:, it_ * 128:(it_ + 1) * 128],
                                             Sf[d_][:, it_ * 128:(it_ + 1) * 128],
                                             dmk[:, d_ * 128:(d_ + 1) * 128],
                                             ALU.add).then_inc(S["vda"], 1)
                    vector.wait_ge(S["tfv"], fb + NJ)
                    vector.tensor_copy(of1[:, :], OO[:, :])
                    vector.tensor_copy(drw[0:1, TC:2 * TC], DDp[0:1, :]).then_inc(S["vof"], 1)
                    vector.drain()
                    vector.wait_ge(S["gql"], 16 * (4 * u + g_ + 1))
                    vector.reciprocal(crow[0:1, :], dc4[0:1, g_ * TC:(g_ + 1) * TC])
                    vector.drain()
                    vector.tensor_tensor(frw[0:1, 0:TC], crow[0:1, :], gch3[0:1, 0:TC], ALU.mult)
                    vector.drain()
                    vector.reciprocal(crow[0:1, :], drw[0:1, TC:2 * TC])
                    vector.drain()
                    vector.tensor_tensor(frw[0:1, TC:2 * TC], crow[0:1, :],
                                         gch3[0:1, TC:2 * TC], ALU.mult)
                    vector.drain()
                    vector.reciprocal(crow[0:1, :], drw[0:1, 0:TC])
                    vector.drain()
                    vector.tensor_tensor(frw[0:1, 2 * TC:3 * TC], crow[0:1, :],
                                         gch3[0:1, 2 * TC:3 * TC],
                                         ALU.mult).then_inc(S["vfr"], 1)
                    cb = 3 * (4 * u + g_)
                    vector.wait_ge(S["tcb"], cb + 1)
                    vector.tensor_tensor(cmb1[:, :], oc4[:, g_ * TC:(g_ + 1) * TC],
                                         FB[:, :], ALU.mult).then_inc(S["vcm"], 1)
                    vector.wait_ge(S["tcb"], cb + 2)
                    vector.tensor_tensor(cmb2[:, :], of1[:, :], FB[:, :],
                                         ALU.mult).then_inc(S["vcm"], 1)
                    vector.drain()
                    vector.tensor_tensor(cmb1[:, :], cmb1[:, :], cmb2[:, :], ALU.add)
                    vector.drain()
                    vector.wait_ge(S["tcb"], cb + 3)
                    vector.tensor_tensor(cmb2[:, :], ow1[:, :], FB[:, :],
                                         ALU.mult).then_inc(S["vcm"], 1)
                    vector.drain()
                    vector.tensor_tensor(cmb1[:, :], cmb1[:, :], cmb2[:, :], ALU.add)
                    vector.drain()
                    if cc >= 1:
                        vector.wait_ge(S["tmm"], 32 * cc)
                    h = 4 * kvh + g_
                    vector.tensor_copy(ocmb[:, h * TC:(h + 1) * TC],
                                       cmb1[:, :]).then_inc(S["vcb"], 1)
                    vector.drain()
            for ot in range(KT):
                widx = cc * KT + ot
                vector.wait_ge(S["sev"], widx + 1)
                vector.wait_ge(xq[widx % 2], 16 * (widx // 2 + 1))
                if widx >= 2:
                    vector.wait_ge(oq[widx % 2], 16 * (widx // 2))
                vector.tensor_tensor(xout[ot % 2][:, :], tmps[ot % 2][:, :],
                                     xres[ot % 2][:, :], ALU.add).then_inc(S["pva"], 1)

    blk.gpsimd(body_gpsimd)
    blk.sync(body_sync)
    blk.tensor(body_tensor)
    blk.scalar(body_scalar)
    blk.vector(body_vector)
    return b, io


def host_s2b_shared(w_o, mem_kv, rko_pair, vto_pair, cko_pair, cvo_pair):
    rkf = np.zeros((1024, 4160), rko_pair[0].dtype)
    vtf = np.zeros((KVH, 128, 33 * 128), vto_pair[0].dtype)
    ckf = np.zeros((128, KVH * 256), cko_pair[0].dtype)
    cvf = np.zeros((256, 1024), cvo_pair[0].dtype)
    for p in range(2):
        rko, vto, cko, cvo = rko_pair[p], vto_pair[p], cko_pair[p], cvo_pair[p]
        for s in range(16):
            t = p + 2 * s
            rkf[:, 64 + 128 * t:64 + 128 * (t + 1)] = rko[:, 128 * s:128 * (s + 1)]
            vtf[:, :, 128 * (t + 1):128 * (t + 2)] = vto[:, :, 128 * s:128 * (s + 1)].reshape(
                KVH, 128, 128)
            for kv in range(KVH):
                ckf[:, kv * 256 + 8 * t:kv * 256 + 8 * (t + 1)] = cko[:, kv * 128 + 8 * s:kv * 128 + 8 * (s + 1)]
                cvf[8 * t:8 * (t + 1), kv * 128:(kv + 1) * 128] = cvo[8 * s:8 * (s + 1), kv * 128:(kv + 1) * 128]
    return {
        "rkf": np.ascontiguousarray(rkf),
        "vtf_t": np.ascontiguousarray(vtf),
        "ckf": np.ascontiguousarray(ckf),
        "cvf": np.ascontiguousarray(cvf),
        "memk_pm": np.ascontiguousarray(mem_kv[0].reshape(KVH, 128).T.astype(np.float32)),
        "memv_row": np.ascontiguousarray(mem_kv[1].reshape(1, KVH * 128).astype(np.float32)),
        "wo_t": np.ascontiguousarray(
            w_o.T.reshape(32, 128, 32, 128).transpose(2, 1, 0, 3).reshape(32, 128, 32 * 128)
            .astype(np.float32)),
    }


def host_s2b_core(par):
    r = np.arange(128)
    s = np.arange(16)
    pos = (128 * (par + 2 * s)[None, :] + r[:, None]).astype(np.float32)
    own = (8 * (par + 2 * s)[None, :] + (r // 16)[:, None]).astype(np.float32)
    dmsel = np.zeros((128, 256), np.float32)
    for jj in range(128):
        for ii in range(128):
            if jj // 16 == ii // 16 and jj > ii:
                dmsel[jj, par * 128 + ii] = NEG
    edsel = np.zeros((128, 192), np.float32)
    if par == 0:
        edsel[:, 0:64] = NEG
    return {
        "pos_pm": np.ascontiguousarray(pos),
        "own_pm": np.ascontiguousarray(own),
        "dmsel": dmsel,
        "edsel": edsel,
    }


def host_s2b_slabs(par, rkf, vtf_t):
    wkf = np.zeros((NTC, 1024, 960), rkf.dtype)
    wvf = np.zeros((NTC, KVH, 128, 1024), vtf_t.dtype)
    for cc in range(NTC):
        o = par * 128 + 1024 * cc
        wkf[cc] = rkf[:, o:o + 960]
        wvf[cc] = vtf_t[:, :, o:o + 1024]
    return {"wkf": np.ascontiguousarray(wkf), "wvf": np.ascontiguousarray(wvf)}

import concourse.bass as bass
from concourse.bass_utils import run_bass_kernel_spmd

_PROG = {}


def _reset(nc):
    nc.reset(previous_local_dram_base=nc.local_dram_base,
             previous_shared_dram_base=nc.shared_dram_base)


def _io_s1(nc, xr, xo):
    io = {"xr": xr, "xo": xo}
    io["qw8_gu"] = nc.dram_tensor("ffnq_qw8_gu", [D, 11008], I8, kind="ExternalInput").ap()
    io["qz8_gu"] = nc.dram_tensor("ffnq_qz8_gu", [32, 11008], I8, kind="ExternalInput").ap()
    io["s_gu"] = nc.dram_tensor("ffnq_s_gu", [32, 22016], F32, kind="ExternalInput").ap()
    io["qw8_dn"] = nc.dram_tensor("ffnq_qw8_dn", [11008, 2048], I8, kind="ExternalInput").ap()
    io["qz8_dn"] = nc.dram_tensor("ffnq_qz8_dn", [86, 2048], I8, kind="ExternalInput").ap()
    io["s_dn"] = nc.dram_tensor("ffnq_s_dn", [86, D], F32, kind="ExternalInput").ap()
    return io


def _io_s2a(nc, xr):
    io = {"xr": xr}
    for nm, shp in [("wqkv_t", [48, 128, 32 * 128]), ("wg_t", [128, 32 * 96]),
                    ("cost", [128, 16 * 64]), ("sint", [128, 16 * 64]),
                    ("kpos_pm", [128, 8 * 16]), ("vpos_pm", [128, 8 * 16]),
                    ("kw1_t", [16, 128, 16 * 128]), ("kw2_t", [1, 128, 16 * 128]),
                    ("vw1_t", [16, 128, 16 * 128]), ("vw2_t", [1, 128, 16 * 128]),
                    ("b1k_pm", [128, 16]), ("b2k_pm", [128, 1]),
                    ("b1v_pm", [128, 16]), ("b2v_pm", [128, 1]), ("bg_f", [96, 1])]:
        io[nm] = nc.dram_tensor("nsa_" + nm, shp, F32, kind="ExternalInput").ap()
    for nm, shp, dt in [("Qraw", [D, TPC], BF16), ("Qrope", [D, TPC], BF16),
                        ("gso", [96, TPC], F32), ("rko", [1024, TPC], BF16),
                        ("vto", [KVH, 128, TPC], BF16), ("cko", [128, KVH * 128], BF16),
                        ("cvo", [128, KVH * 128], BF16)]:
        io[nm] = nc.dram_tensor("a2_" + nm, shp, dt, kind="ExternalOutput").ap()
    return io


def _io_s2b(nc, xr, xo):
    io = {"xr": xr, "xo": xo}
    for nm, shp, dt in [("Qraw", [D, TPC], BF16), ("Qrope", [D, TPC], BF16),
                        ("gso", [96, TPC], F32), ("rkf", [1024, 4160], BF16),
                        ("vtf_t", [KVH, 128, 33 * 128], BF16),
                        ("ckf", [128, KVH * 256], BF16), ("cvf", [256, 1024], BF16),
                        ("memk_pm", [128, KVH], F32), ("memv_row", [1, 1024], F32),
                        ("wo_t", [32, 128, 32 * 128], F32),
                        ("wkf", [NTC, 1024, 960], BF16),
                        ("wvf", [NTC, KVH, 128, 1024], BF16),
                        ("pos_pm", [128, 16], F32), ("own_pm", [128, 16], F32),
                        ("dmsel", [128, 256], F32), ("edsel", [128, 192], F32)]:
        io[nm] = nc.dram_tensor("b2_" + nm, shp, dt, kind="ExternalInput").ap()
    return io


def _io_s4(nc, xr, xo, shared=None):
    if shared is None:
        shared = {}
        shared["WT"] = nc.dram_tensor("l1f_WT", [D, D], F32, kind="ExternalInput").ap()
        shared["AT"] = nc.dram_tensor("l1f_AT", [D, 256], F32, kind="ExternalInput").ap()
        shared["BT"] = nc.dram_tensor("l1f_BT", [256, D], F32, kind="ExternalInput").ap()
        shared["b_pm"] = nc.dram_tensor("l1f_b_pm", [128, KT], F32, kind="ExternalInput").ap()
        shared["p_rep"] = nc.dram_tensor("l1f_p_rep", [128, 2], F32, kind="ExternalInput").ap()
    io = dict(shared)
    io["xr"] = xr
    io["xo"] = xo
    return io


def _io_s5(nc, xr, xo, xedge):
    io = {"xr": xr, "xo": xo, "xedge": xedge}
    for nm, shp in [("cw_t", [24, 128, 3, KT, 128]), ("cb_pm", [128, 24]),
                    ("wq_t", [8, 128, 8, 128]), ("wk_t", [8, 128, 8, 128]),
                    ("wv_t", [8, 128, 8, 128]), ("wo_t", [8, 128, 8, 128]),
                    ("bq_pm", [128, 8]), ("bk_pm", [128, 8]), ("bv_pm", [128, 8]),
                    ("bo_pm", [128, 8]),
                    ("pw_t", [KT, 128, 8, 128]), ("pb_pm", [128, KT])]:
        io[nm] = nc.dram_tensor("l1a_" + nm, shp, F32, kind="ExternalInput").ap()
    return io


def _progA():
    nc = bass.Bass("TRN2", target_bir_lowering=False, debug=False)
    xin = nc.dram_tensor("xin", [D, TPC], F32, kind="ExternalInput").ap()
    xi1 = nc.dram_tensor("xi1", [D, TPC], F32, kind="ExternalOutput").ap()
    b, _ = build_s1(nc, name="s1", io=_io_s1(nc, xin, xi1))
    b.es.close()
    _reset(nc)
    b, _ = build_s2a(nc, name="a2", io=_io_s2a(nc, xi1))
    b.es.close()
    return nc


def _progB():
    nc = bass.Bass("TRN2", target_bir_lowering=False, debug=False)
    xi1 = nc.dram_tensor("xi1", [D, TPC], F32, kind="ExternalInput").ap()
    xi2 = nc.dram_tensor("xi2", [D, TPC], F32, kind="Internal").ap()
    xi3 = nc.dram_tensor("xi3", [D, TPC], F32, kind="Internal").ap()
    xo4 = nc.dram_tensor("xo4", [D, TPC], F32, kind="ExternalOutput").ap()
    b, _ = build_s2b(nc, name="b2", io=_io_s2b(nc, xi1, xi2))
    b.es.close()
    _reset(nc)
    b, io3 = build_s1(nc, name="s3", io=_io_s1(nc, xi2, xi3))
    b.es.close()
    _reset(nc)
    b, _ = build_s4(nc, name="s4", io=_io_s4(nc, xi3, xo4))
    b.es.close()
    return nc


def _progC():
    nc = bass.Bass("TRN2", target_bir_lowering=False, debug=False)
    xo4 = nc.dram_tensor("xo4", [D, TPC], F32, kind="ExternalInput").ap()
    xedge = nc.dram_tensor("s5_xedge", [D, 32], F32, kind="ExternalInput").ap()
    xi5 = nc.dram_tensor("xi5", [D, TPC], F32, kind="Internal").ap()
    xout = nc.dram_tensor("xout", [D, TPC], F32, kind="ExternalOutput").ap()
    b, _ = build_s5(nc, name="s5", io=_io_s5(nc, xo4, xi5, xedge))
    b.es.close()
    _reset(nc)
    b, _ = build_s4(nc, name="s6", io=_io_s4(nc, xi5, xout))
    b.es.close()
    return nc


def _prog(nm):
    if nm not in _PROG:
        _PROG[nm] = {"A": _progA, "B": _progB, "C": _progC}[nm]()
    return _PROG[nm]


def _core_idx(par):
    return np.concatenate([np.arange(128 * (par + 2 * i), 128 * (par + 2 * i) + 128)
                           for i in range(16)])


def _host_s1(inp):
    def perm_i32(q):
        cols = []
        for p in range(86):
            cols.extend(range(16 * p, 16 * p + 16))
            cols.extend(range(1376 + 16 * p, 1376 + 16 * p + 16))
        return q[:, cols]

    def perm_dense(s):
        cols = []
        for p in range(86):
            cols.extend(range(128 * p, 128 * p + 128))
            cols.extend(range(11008 + 128 * p, 11008 + 128 * p + 128))
        return s[:, cols]
    return {
        "ffnq_qw8_gu": np.ascontiguousarray(perm_i32(np.asarray(inp["L0_ffn_qweight_gu"]))).view(np.int8),
        "ffnq_qz8_gu": np.ascontiguousarray(perm_i32(np.asarray(inp["L0_ffn_qzeros_gu"]))).view(np.int8),
        "ffnq_s_gu": np.ascontiguousarray(perm_dense(np.asarray(inp["L0_ffn_scales_gu"], np.float32))),
        "ffnq_qw8_dn": np.ascontiguousarray(np.asarray(inp["L0_ffn_qweight_dn"])).view(np.int8),
        "ffnq_qz8_dn": np.ascontiguousarray(np.asarray(inp["L0_ffn_qzeros_dn"])).view(np.int8),
        "ffnq_s_dn": np.ascontiguousarray(np.asarray(inp["L0_ffn_scales_dn"], np.float32)),
    }


def _host_s2a(inp, par):
    pos = np.concatenate([np.arange(128 * (par + 2 * i), 128 * (par + 2 * i) + 128)
                          for i in range(16)]).astype(np.float64)
    inv = 1.0 / 10000.0 ** (np.arange(0, 128, 2) / 128.0)
    ang = pos[:, None] * inv[None, :]
    cost = np.cos(ang).astype(np.float32).reshape(16, 128, 64).transpose(1, 0, 2).reshape(128, 16 * 64)
    sint = np.sin(ang).astype(np.float32).reshape(16, 128, 64).transpose(1, 0, 2).reshape(128, 16 * 64)
    w_qkv = np.asarray(inp["L0_attn_w_qkv"], np.float32)
    w_gate = np.asarray(inp["L0_attn_w_gate"], np.float32)
    return {
        "nsa_wqkv_t": np.ascontiguousarray(
            w_qkv.T.reshape(32, 128, 48, 128).transpose(2, 1, 0, 3).reshape(48, 128, 32 * 128)),
        "nsa_wg_t": np.ascontiguousarray(
            w_gate.T.reshape(32, 128, 96).transpose(1, 0, 2).reshape(128, 32 * 96)),
        "nsa_cost": np.ascontiguousarray(cost),
        "nsa_sint": np.ascontiguousarray(sint),
        "nsa_kpos_pm": np.ascontiguousarray(
            np.asarray(inp["L0_attn_k_pos"], np.float32).transpose(2, 0, 1).reshape(128, 8 * 16)),
        "nsa_vpos_pm": np.ascontiguousarray(
            np.asarray(inp["L0_attn_v_pos"], np.float32).transpose(2, 0, 1).reshape(128, 8 * 16)),
        "nsa_kw1_t": np.ascontiguousarray(
            np.asarray(inp["L0_attn_k_w1"], np.float32).T.reshape(16, 128, 16, 128)
            .transpose(2, 1, 0, 3).reshape(16, 128, 16 * 128)),
        "nsa_kw2_t": np.ascontiguousarray(
            np.asarray(inp["L0_attn_k_w2"], np.float32).T.reshape(16, 128, 1, 128)
            .transpose(2, 1, 0, 3).reshape(1, 128, 16 * 128)),
        "nsa_vw1_t": np.ascontiguousarray(
            np.asarray(inp["L0_attn_v_w1"], np.float32).T.reshape(16, 128, 16, 128)
            .transpose(2, 1, 0, 3).reshape(16, 128, 16 * 128)),
        "nsa_vw2_t": np.ascontiguousarray(
            np.asarray(inp["L0_attn_v_w2"], np.float32).T.reshape(16, 128, 1, 128)
            .transpose(2, 1, 0, 3).reshape(1, 128, 16 * 128)),
        "nsa_b1k_pm": np.ascontiguousarray(np.asarray(inp["L0_attn_k_b1"], np.float32).reshape(16, 128).T),
        "nsa_b2k_pm": np.ascontiguousarray(np.asarray(inp["L0_attn_k_b2"], np.float32).reshape(1, 128).T),
        "nsa_b1v_pm": np.ascontiguousarray(np.asarray(inp["L0_attn_v_b1"], np.float32).reshape(16, 128).T),
        "nsa_b2v_pm": np.ascontiguousarray(np.asarray(inp["L0_attn_v_b2"], np.float32).reshape(1, 128).T),
        "nsa_bg_f": np.ascontiguousarray(np.asarray(inp["L0_attn_b_gate"], np.float32)[:, None]),
    }


def _host_s4(inp):
    W = np.asarray(inp["L1_ffn_W"], np.float32)
    A = np.asarray(inp["L1_ffn_A"], np.float32)
    Bw = np.asarray(inp["L1_ffn_B"], np.float32)
    bb = np.asarray(inp["L1_ffn_b"], np.float32)
    probs = np.asarray(inp["L1_ffn_probs"], np.float32)
    return {
        "l1f_WT": np.ascontiguousarray(W.T),
        "l1f_AT": np.ascontiguousarray(A.reshape(256, D).T),
        "l1f_BT": np.ascontiguousarray(Bw.transpose(0, 2, 1).reshape(256, D)),
        "l1f_b_pm": np.ascontiguousarray(bb.reshape(KT, 128).T),
        "l1f_p_rep": np.ascontiguousarray(np.repeat(probs, 16).reshape(2, 128).T),
    }


def _make_edge(xp0, xp1, par):
    xp = [xp0, xp1]

    def col(gt):
        if gt < 0 or gt >= 4096:
            return np.zeros(D, np.float32)
        t = gt // 128
        return xp[t % 2][:, (t // 2) * 128 + gt % 128]

    cols = []
    for i in range(16):
        t = par + 2 * i
        cols.append(col(128 * t - 1))
        cols.append(col(128 * (t + 1)))
    return np.ascontiguousarray(np.stack(cols, axis=1))


def kernel(**inputs):
    x0 = np.asarray(inputs["L0_attn_hidden_states"], np.float32)
    idxs = [_core_idx(0), _core_idx(1)]
    xr = [np.ascontiguousarray(x0[c // 2, idxs[c % 2]].T) for c in range(8)]

    h1 = _host_s1(inputs)
    h2a = [_host_s2a(inputs, par) for par in range(2)]
    maps = [dict(h1, **h2a[c % 2], xin=xr[c]) for c in range(8)]
    r = run_bass_kernel_spmd(_prog("A"), maps, core_ids=list(range(8)))
    ra = r.results
    x1 = [np.ascontiguousarray(np.asarray(ra[c]["xi1"], np.float32)) for c in range(8)]

    mem_kv = np.asarray(inputs["L0_attn_mem_kv"], np.float32)
    w_o = np.asarray(inputs["L0_attn_w_o"], np.float32)
    h4 = _host_s4(inputs)
    maps = []
    for b_ in range(4):
        pair = [ra[2 * b_], ra[2 * b_ + 1]]
        shared = host_s2b_shared(
            w_o, mem_kv,
            [np.asarray(p["a2_rko"]) for p in pair],
            [np.asarray(p["a2_vto"]) for p in pair],
            [np.asarray(p["a2_cko"]) for p in pair],
            [np.asarray(p["a2_cvo"]) for p in pair])
        for par in range(2):
            c = 2 * b_ + par
            m = {"b2_" + k: v for k, v in shared.items()}
            m.update({"b2_" + k: v for k, v in host_s2b_core(par).items()})
            m.update({"b2_" + k: v
                      for k, v in host_s2b_slabs(par, shared["rkf"], shared["vtf_t"]).items()})
            m["xi1"] = x1[c]
            m["b2_Qraw"] = np.asarray(ra[c]["a2_Qraw"])
            m["b2_Qrope"] = np.asarray(ra[c]["a2_Qrope"])
            m["b2_gso"] = np.asarray(ra[c]["a2_gso"])
            m.update(h1)
            m.update(h4)
            maps.append(m)
    r = run_bass_kernel_spmd(_prog("B"), maps, core_ids=list(range(8)))
    x4 = [np.ascontiguousarray(np.asarray(r.results[c]["xo4"], np.float32)) for c in range(8)]

    h5 = host_s5(inputs)
    maps = []
    for c in range(8):
        b_, par = c // 2, c % 2
        maps.append(dict(h5, **h4, xo4=x4[c],
                         s5_xedge=_make_edge(x4[2 * b_], x4[2 * b_ + 1], par)))
    r = run_bass_kernel_spmd(_prog("C"), maps, core_ids=list(range(8)))
    x6 = [np.asarray(r.results[c]["xout"], np.float32) for c in range(8)]

    out = np.empty((4, 4096, 4096), np.float32)
    for c in range(8):
        out[c // 2, idxs[c % 2]] = x6[c].T
    return out
